# version 06
# baseline (speedup 1.0000x reference)
import numpy as np
import ml_dtypes
from contextlib import ExitStack

import concourse.bass as bass
import concourse.mybir as mybir
from concourse.bass_utils import run_bass_kernel_spmd

F32 = mybir.dt.float32
BF16 = mybir.dt.bfloat16
FP8 = mybir.dt.float8e4
AF = mybir.ActivationFunctionType
ALU = mybir.AluOpType
AX = mybir.AxisListType

D = 1024
H = 4
HV = 512
C = 128
EPS = 1e-5

ENGS = ("sync", "gpsimd", "vector", "scalar", "tensor")


class Em:

    def __init__(self, cnt=None):
        self.cmds = {e: [] for e in ENGS}
        self.cnt = dict(cnt) if cnt else {}

    def val(self, sem):
        return self.cnt.get(sem, 0)

    def op(self, eng, build_fn, waits=(), inc=None):
        waits = tuple(waits)

        def cmd(engine, sems):
            for sname, v in waits:
                if v > 0:
                    engine.wait_ge(sems[sname], v)
            instr = build_fn(engine)
            if inc is not None and instr is not None:
                instr.then_inc(sems[inc[0]], inc[1])
        self.cmds[eng].append(cmd)
        if inc is not None:
            self.cnt[inc[0]] = self.cnt.get(inc[0], 0) + inc[1]
            return self.cnt[inc[0]]
        return None

    def replay_block(self, nc, sems):
        with nc.Block() as block:
            for name in ENGS:
                def mk(n):
                    def run(engine):
                        for cmd in self.cmds[n]:
                            cmd(engine, sems)
                    return run
                getattr(block, name)(mk(name))


def _gammas():
    return [np.float32(1.0) - np.float32(2.0 ** (-5.0 - h)) for h in range(H)]


def host_tables(T):
    j = np.arange(0, 256, 2, dtype=np.float32) / np.float32(256)
    inv = 1.0 / (10000.0 ** j.astype(np.float64))
    freqs = (np.arange(T, dtype=np.float32)[:, None]
             * inv.astype(np.float32)[None, :]).astype(np.float32)
    cos64 = np.cos(freqs.astype(np.float64))
    sin64 = np.sin(freqs.astype(np.float64))
    i_loc = (np.arange(T) % C).astype(np.float64)
    blocks = []
    for kind in range(4):
        for h in range(H):
            g = np.float64(_gammas()[h])
            if kind < 2:
                scale = (g ** i_loc) / 16.0
                base = cos64 if kind == 0 else sin64
            else:
                scale = g ** (-i_loc)
                base = cos64 if kind == 2 else sin64
            blocks.append((base * scale[:, None]).astype(np.float32))
    return np.ascontiguousarray(np.concatenate(blocks, axis=1))


_STAGES = ("w", "b1", "b2", "b3", "b4", "ba", "bb", "bc", "base", "proj", "vg", "tq", "ret", "wo", "cc", "full")


def build(n_tiles, n_cores, stage="full", dbg=False):
    sidx = _STAGES.index(stage)

    def sge(name):
        return sidx >= _STAGES.index(name)

    T = n_tiles * C
    nc = bass.Bass("TRN2", target_bir_lowering=False,
                   detect_race_conditions=False)

    x_d = nc.dram_tensor("x", [T, D], F32, kind="ExternalInput")
    wqt_d = nc.dram_tensor("wqt", [D, D], F32, kind="ExternalInput")
    wkt_d = nc.dram_tensor("wkt", [D, D], F32, kind="ExternalInput")
    wvt_d = nc.dram_tensor("wvt", [D, 2 * D], F32, kind="ExternalInput")
    wgt_d = nc.dram_tensor("wgt", [D, 2 * D], F32, kind="ExternalInput")
    wot_d = nc.dram_tensor("wot", [2 * D, D], F32, kind="ExternalInput")
    wft_d = nc.dram_tensor("wft", [D, D], F32, kind="ExternalInput")
    bias_d = nc.dram_tensor("fbias", [1, D], F32, kind="ExternalInput")
    gnw_d = nc.dram_tensor("gnw", [128, 4], F32, kind="ExternalInput")
    idf_d = nc.dram_tensor("idf", [128, 128], F32, kind="ExternalInput")
    mskt_d = nc.dram_tensor("mskt", [128, 128], F32, kind="ExternalInput")
    tabs_d = nc.dram_tensor("tabs", [T, 2048], F32, kind="ExternalInput")

    out_d = nc.dram_tensor("out", [T, D], F32, kind="ExternalOutput")

    scr_kind = "ExternalOutput" if dbg else "Internal"
    xnt_scr = nc.dram_tensor("xnt_scr", [T, D], F32, kind=scr_kind)
    s_scr = nc.dram_tensor("s_scr", [T, D], BF16, kind=scr_kind)
    if dbg:
        xh_scr = nc.dram_tensor("xh_scr", [T, D], BF16, kind="ExternalOutput")
        xl_scr = nc.dram_tensor("xl_scr", [T, D], BF16, kind="ExternalOutput")
    cc_in = nc.dram_tensor("cc_in", [1, 1], F32)
    cc_out = nc.dram_tensor("cc_out", [1, 1], F32)
    red_scr = nc.dram_tensor("red_scr", [128, 1], F32)
    red_scr2 = nc.dram_tensor("red_scr2", [128, 1], F32)
    wmx_scr = nc.dram_tensor("wmx_scr", [1, 1], F32)

    gams = _gammas()
    gCs = [float(np.float32(np.float64(g) ** C)) for g in gams]

    with ExitStack() as top:
        e = top.enter_context
        idb = e(nc.sbuf_tensor("idb_sb", [128, 128], BF16))
        mskt = e(nc.sbuf_tensor("mskt_sb", [128, 128], BF16))
        gnw = e(nc.sbuf_tensor("gnw_sb", [128, 4], F32))
        amaxb = e(nc.sbuf_tensor("amaxb", [128, max(n_tiles, 2)], F32))
        stats = e(nc.sbuf_tensor("stats", [128, 16], F32))
        gvar = e(nc.sbuf_tensor("gvar", [128, 4], F32))
        gv4 = e(nc.sbuf_tensor("gv4", [128, 4], F32))
        rtg = e(nc.sbuf_tensor("rtg", [128, 4], F32))
        rsg = e(nc.sbuf_tensor("rsg", [128, 4], F32))
        rowbuf = e(nc.sbuf_tensor("rowbuf", [1, 128], F32))

        sems = {}
        for s in ("dxa", "dcs", "dx0", "dx1", "dt0", "dt1", "dsx", "dss",
                  "dsc", "dgp", "da0", "da1", "db0", "db1", "dp0", "dp1",
                  "do0", "do1", "sv", "ss", "st", "sq"):
            sems[s] = e(nc.semaphore(s))

        em = Em()
        lm = {}

        SEMR = range(155, 256)
        em.op("gpsimd", lambda eng: eng.sem_clear(SEMR))
        em.op("gpsimd", (lambda eng: eng.dma_start(
            rowbuf[0:1, 0:1], idf_d[0:1, 0:1])), inc=("dgp", 16))
        for _eng in ("sync", "vector", "scalar", "tensor"):
            em.op(_eng, lambda eng: None, waits=(("dgp", 16),))

        def lw(sem, key):
            return [(sem, lm[key])] if key in lm else []

        with ExitStack() as p1:
            e1 = p1.enter_context
            wq = e1(nc.sbuf_tensor("wq_sb", [128, 8 * D], BF16))
            wk = e1(nc.sbuf_tensor("wk_sb", [128, 8 * D], BF16))
            wv = e1(nc.sbuf_tensor("wv_sb", [128, 16 * D], BF16))
            wg = e1(nc.sbuf_tensor("wg_sb", [128, 16 * D], BF16))
            wo = e1(nc.sbuf_tensor("wo_sb", [128, 16 * D], BF16))
            Tst = e1(nc.sbuf_tensor("Tst", [128, 4096], F32))
            xb = [e1(nc.sbuf_tensor(f"xb{i}", [128, D], F32)) for i in range(2)]
            tb = [e1(nc.sbuf_tensor("tb0", [128, 2048], F32))]
            xn = e1(nc.sbuf_tensor("xn", [128, D], F32))
            xntb = e1(nc.sbuf_tensor("xntb", [128, D], BF16))
            xh = e1(nc.sbuf_tensor("xh", [128, D], BF16))
            qp = e1(nc.sbuf_tensor("qp", [128, D], BF16))
            kp = e1(nc.sbuf_tensor("kp", [128, D], BF16))
            qpT = e1(nc.sbuf_tensor("qpT", [128, D], BF16))
            kpT = e1(nc.sbuf_tensor("kpT", [128, D], BF16))
            vb = e1(nc.sbuf_tensor("vb", [128, 2 * D], BF16))
            mb = e1(nc.sbuf_tensor("mb", [128, 2 * D], BF16))
            Sbf = e1(nc.sbuf_tensor("Sbf", [128, 1024], BF16))
            atb = e1(nc.sbuf_tensor("atb", [128, 128], BF16))
            obf = e1(nc.sbuf_tensor("obf", [128, 2 * D], BF16))
            ogb = e1(nc.sbuf_tensor("ogb", [128, 2 * D], BF16))
            ogT = e1(nc.sbuf_tensor("ogT", [128, 2 * D], BF16))
            sbo = e1(nc.sbuf_tensor("sbo", [128, D], BF16))
            tmp1 = e1(nc.sbuf_tensor("tmp1", [128, 512], BF16))
            tmp2 = e1(nc.sbuf_tensor("tmp2", [128, 512], BF16))
            junk = e1(nc.sbuf_tensor("junk", [128, D], BF16))
            junk2 = e1(nc.sbuf_tensor("junk2", [128, HV], BF16))

            pA = e1(nc.psum_tensor("pA", [128, D], F32))
            pB = e1(nc.psum_tensor("pB", [128, D], F32))
            pC = e1(nc.psum_tensor("pC", [128, D], F32))
            pD = e1(nc.psum_tensor("pD", [128, D], F32))

            def wload(dst, src_d, nslab, ncol):
                for s in range(nslab):
                    em.op("gpsimd",
                          (lambda eng, d=dst, sd=src_d, s_=s, nco=ncol:
                           eng.dma_start(d[:, s_ * nco:(s_ + 1) * nco],
                                         sd[s_ * 128:(s_ + 1) * 128, :])),
                          inc=("dgp", 16))
            wload(wq, wqt_d, 8, D)
            wload(wk, wkt_d, 8, D)
            wload(wv, wvt_d, 8, 2 * D)
            wload(wg, wgt_d, 8, 2 * D)
            wload(wo, wot_d, 16, D)
            em.op("gpsimd", lambda eng: eng.dma_start(idb[:, :], idf_d[:, :]),
                  inc=("dgp", 16))
            em.op("gpsimd", lambda eng: eng.dma_start(mskt[:, :], mskt_d[:, :]),
                  inc=("dgp", 16))
            w_loaded = em.val("dgp")
            em.op("sync", lambda eng: eng.dma_start(gnw[:, :], gnw_d[:, :]),
                  inc=("dcs", 16))
            const_xa = em.val("dcs")

            gnw_done = None
            for s in range(16):
                gnw_done = em.op(
                    "vector",
                    (lambda eng, s_=s: eng.tensor_scalar(
                        wo[:, s_ * D:(s_ + 1) * D], wo[:, s_ * D:(s_ + 1) * D],
                        gnw[:, (s_ % 4):(s_ % 4) + 1], None, ALU.mult)),
                    waits=((("dgp", w_loaded), ("dcs", const_xa))
                           if s == 0 else ()),
                    inc=("sv", 1))
            tst_init = em.op("vector", lambda eng: eng.memset(Tst[:, :], 0.0),
                             inc=("sv", 1))

            def emit_ln(t):
                xcur = xb[t % 2]
                r0 = t * 128
                w_x = lw("ss", ("xfree", t - 2))
                xsem = f"dx{t % 2}"
                xin = em.op("sync",
                            (lambda eng, xc=xcur, r=r0:
                             eng.dma_start(xc[:, :], x_d[r:r + 128, :])),
                            waits=w_x, inc=(xsem, 16))

                sum_done = em.op("vector",
                                 (lambda eng, xc=xcur: eng.reduce_sum(
                                     stats[:, 0:1], xc[:, :], AX.X)),
                                 waits=((xsem, xin),), inc=("sv", 1))
                sq_done = em.op("scalar",
                                (lambda eng, xc=xcur: eng.activation(
                                    junk[:, :], xc[:, :], AF.Square,
                                    accum_out=stats[:, 1:2])),
                                waits=[(xsem, xin)]
                                + lw("st", ("xnt_mm", t - 1)),
                                inc=("ss", 1))
                c1 = em.op("vector", lambda eng: eng.tensor_scalar(
                    stats[:, 2:3], stats[:, 0:1], 1.0 / D, None, ALU.mult),
                    waits=(("sv", sum_done),), inc=("sv", 1))
                c2 = em.op("vector", lambda eng: eng.tensor_tensor(
                    stats[:, 13:14], stats[:, 2:3], stats[:, 2:3], ALU.mult),
                    waits=(("sv", c1),), inc=("sv", 1))
                c3 = em.op("vector", lambda eng: eng.scalar_tensor_tensor(
                    stats[:, 3:4], stats[:, 1:2], 1.0 / D, stats[:, 13:14],
                    ALU.mult, ALU.subtract),
                    waits=(("ss", sq_done), ("sv", c2)), inc=("sv", 1))
                var_done = em.op("vector", lambda eng: eng.tensor_scalar(
                    stats[:, 3:4], stats[:, 3:4], EPS, None, ALU.add),
                    waits=(("sv", c3),), inc=("sv", 1))
                rtv_done = em.op("scalar", lambda eng: eng.activation(
                    stats[:, 4:5], stats[:, 3:4], AF.Sqrt),
                    waits=(("sv", var_done),), inc=("ss", 1))
                c5 = em.op("vector", lambda eng: eng.reciprocal(
                    stats[:, 5:6], stats[:, 4:5]), waits=(("ss", rtv_done),),
                    inc=("sv", 1))
                nmr_done = em.op("vector", lambda eng: eng.scalar_tensor_tensor(
                    stats[:, 6:7], stats[:, 2:3], -1.0, stats[:, 5:6],
                    ALU.mult, ALU.mult), waits=(("sv", c5),), inc=("sv", 1))
                w_xn = ([("sv", nmr_done)] + lw("st", ("xnt_mm", t - 1))
                        + lw("dsx", ("xnts", t - 1))
                        + lw("sv", ("xntf_ev", t - 1)))
                xn_done = em.op("scalar", (lambda eng, xc=xcur: eng.activation(
                    xn[:, :], xc[:, :], AF.Identity,
                    bias=stats[:, 6:7], scale=stats[:, 5:6])),
                    waits=w_xn, inc=("ss", 1))
                lm[("xfree", t)] = xn_done
                em.op("vector", (lambda eng, t_=t: eng.reduce_max(
                    amaxb[:, t_:t_ + 1], xn[:, :], AX.X,
                    apply_absolute_value=True)),
                    waits=(("ss", xn_done),), inc=("sv", 1))

                hl_done = em.op("vector", lambda eng: eng.tensor_copy(
                    xh[:, :], xn[:, :]), waits=(("ss", xn_done),))
                hl_done = em.op("vector", lambda eng: eng.tensor_tensor(
                    junk[:, :], xn[:, :], xh[:, :], ALU.subtract),
                    inc=("sv", 1))
                lm[("hl", t)] = hl_done

            if sge("b1"):
                emit_ln(0)

            for t in range(n_tiles if sge("b1") else 0):
                tcur = tb[0]
                r0 = t * 128

                w_t = lw("sv", ("tfree", t - 1))
                tsem = f"dt{t % 2}"
                tin = em.op("sync",
                            (lambda eng, tc=tcur, r=r0:
                             eng.dma_start(tc[:, :], tabs_d[r:r + 128, :])),
                            waits=w_t, inc=(tsem, 16))

                if not sge("bb"):
                    continue
                hl_done = lm[("hl", t)]
                if dbg:
                    em.op("gpsimd", (lambda eng, r=r0: eng.dma_start(
                        xh_scr[r:r + 128, :], xh[:, :])),
                        waits=(("sv", hl_done),), inc=("dgp", 16))
                    em.op("gpsimd", (lambda eng, r=r0: eng.dma_start(
                        xl_scr[r:r + 128, :], junk[:, :])),
                        waits=(("sv", hl_done),), inc=("dgp", 16))
                w_tr = ([("sv", hl_done), ("dgp", w_loaded)]
                        + lw("ss", ("o_ev", t - 1, H - 1))
                        + lw("sv", ("at_ev", t - 1, H - 1))
                        + lw("ss", ("xnt_ev", t - 1))
                        + lw("sv", ("xntf_ev", t - 1)))
                for s in range(8):
                    em.op("tensor", (lambda eng, s_=s: eng.matmul(
                        pA[:, s_ * 128:(s_ + 1) * 128],
                        xh[:, s_ * 128:(s_ + 1) * 128], idb[:, :],
                        start=True, stop=False)),
                        waits=(w_tr if s == 0 else ()))
                    em.op("tensor", (lambda eng, s_=s: eng.matmul(
                        pA[:, s_ * 128:(s_ + 1) * 128],
                        junk[:, s_ * 128:(s_ + 1) * 128], idb[:, :],
                        start=False, stop=True)),
                        inc=("st", 1) if s == 7 else None)
                lm[("xnt_mm", t)] = em.val("st")
                if not sge("bc"):
                    continue
                w_evf = ([("st", lm[("xnt_mm", t)])]
                         + lw("dsx", ("xnts", t - 1)))
                xntf_ev = em.op("vector", lambda eng: eng.tensor_copy(
                    xn[:, :], pA[:, :]), waits=w_evf, inc=("sv", 1))
                lm[("xntf_ev", t)] = xntf_ev
                w_ev = ([("st", lm[("xnt_mm", t)]), ("sv", xntf_ev)]
                        + lw("st", ("g_mm", t - 1)))
                xnt_ev = em.op("scalar", lambda eng: eng.copy(
                    xntb[:, :], pA[:, :]), waits=w_ev, inc=("ss", 1))
                lm[("xnt_ev", t)] = xnt_ev
                if not sge("base"):
                    continue
                lm[("xnts", t)] = em.op("scalar",
                                        (lambda eng, r=r0: eng.dma_start(
                                            xnt_scr[r:r + 128, :],
                                            xn[:, :])),
                                        waits=(("sv", xntf_ev),),
                                        inc=("dsx", 16))

                if not sge("proj"):
                    continue
                def proj_round(wsrc, col0, name_mm, waits_first,
                               dlo=None, dhi=None):
                    dlo = dlo if dlo is not None else pB
                    dhi = dhi if dhi is not None else pC
                    for s in range(8):
                        base = col0 + s * (wsrc.shape[1] // 8)
                        for blk in range(4):
                            dst = (dlo if blk < 2 else dhi)
                            dcol = (blk % 2) * 512
                            em.op("tensor",
                                  (lambda eng, s_=s, b_=base, bl=blk, d=dst,
                                   dc=dcol, w_=wsrc: eng.matmul(
                                       d[:, dc:dc + 512],
                                       xntb[:, s_ * 128:(s_ + 1) * 128],
                                       w_[:, b_ + bl * 512:b_ + bl * 512 + 512],
                                       start=(s_ == 0), stop=(s_ == 7))),
                                  waits=(waits_first if (s == 0 and blk == 0)
                                         else ()),
                                  inc=(("st", 1) if (s == 7 and blk == 3)
                                       else None))
                    return em.val("st")

                w_qk = [("ss", xnt_ev)]
                if t == 0:
                    w_qk += [("dgp", w_loaded), ("sv", gnw_done)]
                else:
                    w_qk += lw("sv", ("s_ev", t - 1))
                for s in range(8):
                    for blk in range(4):
                        dst = pB if blk < 2 else pC
                        wsrc = wq if blk < 2 else wk
                        dcol = (blk % 2) * 512
                        scol = (blk % 2) * 512
                        em.op("tensor",
                              (lambda eng, s_=s, bl=blk, d=dst, dc=dcol,
                               w_=wsrc, sc=scol: eng.matmul(
                                   d[:, dc:dc + 512],
                                   xntb[:, s_ * 128:(s_ + 1) * 128],
                                   w_[:, s_ * D + sc:s_ * D + sc + 512],
                                   start=(s_ == 0), stop=(s_ == 7))),
                              waits=(w_qk if (s == 0 and blk == 0) else ()),
                              inc=(("st", 1) if (s == 7 and blk == 3)
                                   else None))
                qk_mm = em.val("st")

                first_w = [("st", qk_mm), (tsem, tin), ("dcs", const_xa)]
                seq = [
                    (pB, qp, 0, 512, 0, 512, ALU.subtract, 0),
                    (pB, qp, 512, 0, 0, 512, ALU.add, 512),
                    (pC, kp, 0, 512, 1024, 1536, ALU.subtract, 0),
                    (pC, kp, 512, 0, 1024, 1536, ALU.add, 512),
                ]
                n_r = 0
                for (ps, dst, ac, bc, ct, st_, alu, oc) in seq:
                    em.op("vector", (lambda eng, p_=ps, a_=ac, c_=ct:
                                     eng.tensor_tensor(
                                         tmp1[:, :], p_[:, a_:a_ + 512],
                                         tcur[:, c_:c_ + 512], ALU.mult)),
                          waits=(first_w if n_r == 0 else ()))
                    n_r += 1
                    em.op("vector", (lambda eng, p_=ps, b_=bc, s2=st_:
                                     eng.tensor_tensor(
                                         tmp2[:, :], p_[:, b_:b_ + 512],
                                         tcur[:, s2:s2 + 512], ALU.mult)))
                    em.op("vector", (lambda eng, d=dst, o_=oc, a2=alu:
                                     eng.tensor_tensor(
                                         d[:, o_:o_ + 512], tmp1[:, :],
                                         tmp2[:, :], a2)),
                          inc=("sv", 1) if n_r == 4 else None)
                rope_done = em.val("sv")
                lm[("rope_done", t)] = rope_done
                lm[("tfree", t)] = rope_done

                if not sge("vg"):
                    continue
                w_v = ([("ss", xnt_ev), ("sv", xntf_ev)]
                       + lw("sv", ("s_ev", t - 1)))
                v_mm = proj_round(wv, 0, "v", w_v, dlo=pA, dhi=pD)
                w_vev = [("st", v_mm)] + lw("sv", ("og_done", t - 1))
                em.op("scalar", lambda eng: eng.copy(vb[:, 0:1024], pA[:, :]),
                      waits=w_vev)
                v_ev = em.op("scalar", lambda eng: eng.copy(
                    vb[:, 1024:2048], pD[:, :]), inc=("ss", 1))
                g_mm = proj_round(wg, 0, "g", [("ss", v_ev)], dlo=pA, dhi=pD)
                lm[("g_mm", t)] = g_mm
                w_m = [("st", g_mm)] + lw("sv", ("og_done", t - 1))
                em.op("scalar", lambda eng: eng.activation(
                    mb[:, 0:1024], pA[:, :], AF.Silu), waits=w_m)
                m_done = em.op("scalar", lambda eng: eng.activation(
                    mb[:, 1024:2048], pD[:, :], AF.Silu), inc=("ss", 1))
                lm[("m_done", t)] = m_done

                if not sge("tq"):
                    continue
                w_tq = [("sv", rope_done), ("dgp", w_loaded)]
                for s in range(8):
                    em.op("tensor", (lambda eng, s_=s: eng.matmul(
                        pB[:, s_ * 128:(s_ + 1) * 128],
                        qp[:, s_ * 128:(s_ + 1) * 128], idb[:, :],
                        start=True, stop=True)),
                        waits=(w_tq if s == 0 else ()))
                for s in range(8):
                    em.op("tensor", (lambda eng, s_=s: eng.matmul(
                        pC[:, s_ * 128:(s_ + 1) * 128],
                        kp[:, s_ * 128:(s_ + 1) * 128], idb[:, :],
                        start=True, stop=True)),
                        inc=("st", 1) if s == 7 else None)
                tqk_mm = em.val("st")
                em.op("scalar", lambda eng: eng.copy(qpT[:, :], pB[:, :]),
                      waits=(("st", tqk_mm),))
                qkT_ev = em.op("scalar", lambda eng: eng.copy(
                    kpT[:, :], pC[:, :]), inc=("ss", 1))

                em.op("vector", lambda eng: eng.memset(gvar[:, :], 0.0),
                      waits=(("ss", qkT_ev),))

                if not sge("ret"):
                    continue
                for h in range(H):
                    gC = gCs[h]
                    w_P = [("ss", qkT_ev), ("ss", m_done)]
                    if h > 0:
                        w_P.append(("sv", lm[("tupd", t, h - 1)]))
                    em.op("tensor", (lambda eng, h_=h: eng.matmul(
                        pD[:, 0:512], kp[:, 128 * h_:128 * h_ + 128],
                        vb[:, 512 * h_:512 * h_ + 512],
                        start=True, stop=True)), waits=w_P)
                    em.op("tensor", (lambda eng, h_=h: eng.matmul(
                        pD[:, 512:1024],
                        kp[:, 512 + 128 * h_:512 + 128 * h_ + 128],
                        vb[:, 512 * h_:512 * h_ + 512],
                        start=True, stop=True)), inc=("st", 1))
                    P_mm = em.val("st")
                    w_at = []
                    if h > 0:
                        w_at.append(("sv", lm[("at_ev", t, h - 1)]))
                    em.op("tensor", (lambda eng, h_=h: eng.matmul(
                        pA[:, 0:128], kpT[:, 128 * h_:128 * h_ + 128],
                        qpT[:, 128 * h_:128 * h_ + 128],
                        start=True, stop=False)), waits=w_at)
                    em.op("tensor", (lambda eng, h_=h: eng.matmul(
                        pA[:, 0:128], kpT[:, 128 * (4 + h_):128 * (4 + h_) + 128],
                        qpT[:, 128 * (4 + h_):128 * (4 + h_) + 128],
                        start=False, stop=True)), inc=("st", 1))
                    at_mm = em.val("st")
                    w_sb = [("sv", tst_init)] if (t, h) == (0, 0) else \
                        [("st", lm[("o_mm", t, h - 1)] if h > 0
                          else lm[("o_mm", t - 1, H - 1)])]
                    em.op("scalar", (lambda eng, h_=h, g_=gC: eng.activation(
                        Sbf[:, 0:512], Tst[:, 1024 * h_:1024 * h_ + 512],
                        AF.Identity, scale=float(g_))), waits=w_sb)
                    sbf_done = em.op("scalar",
                                     (lambda eng, h_=h, g_=gC: eng.activation(
                                         Sbf[:, 512:1024],
                                         Tst[:, 1024 * h_ + 512:1024 * (h_ + 1)],
                                         AF.Identity, scale=float(g_))),
                                     inc=("ss", 1))
                    em.op("vector", (lambda eng, h_=h, g_=gC:
                                     eng.scalar_tensor_tensor(
                                         Tst[:, 1024 * h_:1024 * h_ + 512],
                                         Tst[:, 1024 * h_:1024 * h_ + 512],
                                         g_, pD[:, 0:512],
                                         ALU.mult, ALU.add)),
                          waits=(("st", P_mm), ("ss", sbf_done)))
                    lm[("tupd", t, h)] = em.op(
                        "vector", (lambda eng, h_=h, g_=gC:
                                   eng.scalar_tensor_tensor(
                                       Tst[:, 1024 * h_ + 512:1024 * (h_ + 1)],
                                       Tst[:, 1024 * h_ + 512:1024 * (h_ + 1)],
                                       g_, pD[:, 512:1024],
                                       ALU.mult, ALU.add)), inc=("sv", 1))
                    lm[("at_ev", t, h)] = em.op(
                        "vector", (lambda eng: eng.tensor_tensor(
                            atb[:, :], pA[:, 0:128], mskt[:, :], ALU.mult)),
                        waits=(("st", at_mm),), inc=("sv", 1))
                    w_o = [("ss", sbf_done), ("sv", lm[("at_ev", t, h)])]
                    if h > 0:
                        w_o.append(("ss", lm[("o_ev", t, h - 1)]))
                    em.op("tensor", (lambda eng, h_=h: eng.matmul(
                        pA[:, 512:1024], qpT[:, 128 * h_:128 * h_ + 128],
                        Sbf[:, 0:512], start=True, stop=False)), waits=w_o)
                    em.op("tensor", (lambda eng, h_=h: eng.matmul(
                        pA[:, 512:1024],
                        qpT[:, 128 * (4 + h_):128 * (4 + h_) + 128],
                        Sbf[:, 512:1024], start=False, stop=False)))
                    em.op("tensor", (lambda eng, h_=h: eng.matmul(
                        pA[:, 512:1024], atb[:, :],
                        vb[:, 512 * h_:512 * h_ + 512],
                        start=False, stop=True)), inc=("st", 1))
                    lm[("o_mm", t, h)] = em.val("st")
                    em.op("scalar", (lambda eng, h_=h: eng.activation(
                        junk2[:, :], pA[:, 512:1024], AF.Square,
                        accum_out=gvar[:, h_:h_ + 1])),
                        waits=(("st", lm[("o_mm", t, h)]),))
                    w_oev = (lw("sv", ("og_done", t - 1))
                             if h == 0 else [])
                    lm[("o_ev", t, h)] = em.op(
                        "scalar", (lambda eng, h_=h: eng.copy(
                            obf[:, 512 * h_:512 * h_ + 512],
                            pA[:, 512:1024])), waits=w_oev, inc=("ss", 1))
                lm[("qkT_free", t)] = lm[("at_ev", t, H - 1)]

                if not sge("wo"):
                    continue
                gv_done = em.op("vector", lambda eng: eng.tensor_scalar(
                    gv4[:, :], gvar[:, :], 1.0 / HV, EPS, ALU.mult, ALU.add),
                    waits=(("ss", lm[("o_ev", t, H - 1)]),), inc=("sv", 1))
                rtg_done = em.op("scalar", lambda eng: eng.activation(
                    rtg[:, :], gv4[:, :], AF.Sqrt),
                    waits=(("sv", gv_done),), inc=("ss", 1))
                rsg_done = em.op("vector", lambda eng: eng.reciprocal(
                    rsg[:, :], rtg[:, :]), waits=(("ss", rtg_done),),
                    inc=("sv", 1))
                if t + 1 < n_tiles:
                    emit_ln(t + 1)
                og_done = None
                for h in range(H):
                    og_done = em.op(
                        "vector", (lambda eng, h_=h: eng.scalar_tensor_tensor(
                            ogb[:, 512 * h_:512 * h_ + 512],
                            obf[:, 512 * h_:512 * h_ + 512],
                            rsg[:, h_:h_ + 1],
                            mb[:, 512 * h_:512 * h_ + 512],
                            ALU.mult, ALU.mult)),
                        waits=((("sv", rsg_done), ("ss", m_done))
                               if h == 0 else ()),
                        inc=("sv", 1) if h == H - 1 else None)
                lm[("og_done", t)] = og_done

                w_ogt = [("sv", og_done), ("ss", m_done),
                         ("sv", lm[("tupd", t, H - 1)])]
                for s in range(16):
                    dst = pB if s < 8 else pC
                    em.op("tensor", (lambda eng, s_=s, d=dst: eng.matmul(
                        d[:, (s_ % 8) * 128:(s_ % 8) * 128 + 128],
                        ogb[:, s_ * 128:(s_ + 1) * 128], idb[:, :],
                        start=True, stop=True)),
                        waits=(w_ogt if s == 0 else ()),
                        inc=("st", 1) if s == 15 else None)
                lm[("ogt_mm", t)] = em.val("st")
                em.op("scalar", lambda eng: eng.copy(ogT[:, 0:1024], pB[:, :]),
                      waits=(("st", lm[("ogt_mm", t)]),))
                ogt_ev = em.op("scalar", lambda eng: eng.copy(
                    ogT[:, 1024:2048], pC[:, :]), inc=("ss", 1))
                lm[("ogt_ev", t)] = ogt_ev
                w_s = [("ss", ogt_ev)] + lw("dss", ("s_store", t - 2))
                for s in range(16):
                    em.op("tensor", (lambda eng, s_=s: eng.matmul(
                        pD[:, 0:512], ogT[:, s_ * 128:(s_ + 1) * 128],
                        wo[:, s_ * D:s_ * D + 512],
                        start=(s_ == 0), stop=(s_ == 15))),
                        waits=(w_s if s == 0 else ()))
                    em.op("tensor", (lambda eng, s_=s: eng.matmul(
                        pD[:, 512:1024], ogT[:, s_ * 128:(s_ + 1) * 128],
                        wo[:, s_ * D + 512:(s_ + 1) * D],
                        start=(s_ == 0), stop=(s_ == 15))),
                        inc=("st", 1) if s == 15 else None)
                lm[("s_mm", t)] = em.val("st")
                lm[("s_ev", t)] = em.op(
                    "vector", (lambda eng: eng.tensor_copy(
                        sbo[:, :], pD[:, :])),
                    waits=[("st", lm[("s_mm", t)])]
                    + lw("dss", ("s_store", t - 1)), inc=("sv", 1))
                lm[("s_store", t)] = em.op(
                    "scalar", (lambda eng, r=r0: eng.dma_start(
                        s_scr[r:r + 128, :], sbo[:, :])),
                    waits=(("sv", lm[("s_ev", t)]),), inc=("dss", 16))


            do_cc = sge("cc")
            am_red = (em.op("vector", (lambda eng: eng.reduce_max(
                stats[:, 8:9], amaxb[:, 0:n_tiles], AX.X)), inc=("sv", 1))
                if do_cc else None)
            g1 = (em.op("gpsimd", (lambda eng: eng.dma_start(
                red_scr[:, :], stats[:, 8:9])),
                waits=(("sv", am_red),), inc=("dgp", 16)) if do_cc else None)
            g2 = (em.op("gpsimd", (lambda eng: eng.dma_start(
                rowbuf[:, :], bass.AP(red_scr, 0, [[1, 1], [1, 128]]))),
                waits=(("dgp", g1),), inc=("dgp", 16)) if do_cc else None)
            am_sc = (em.op("vector", (lambda eng: eng.reduce_max(
                stats[0:1, 8:9], rowbuf[:, :], AX.X)),
                waits=(("dgp", g2),), inc=("sv", 1)) if do_cc else None)
            cc_st = (em.op("gpsimd", (lambda eng: eng.dma_start(
                cc_in[:, :], stats[0:1, 8:9])),
                waits=(("sv", am_sc),), inc=("dgp", 16)) if do_cc else None)
            cc_done = (em.op("gpsimd", (lambda eng: eng.collective_compute(
                "AllReduce", ALU.max,
                replica_groups=[list(range(n_cores))],
                ins=[cc_in.ap().opt()], outs=[cc_out.ap().opt()])),
                waits=(("dgp", cc_st),), inc=("sq", 1)) if do_cc else None)

            p1_final = [(k, em.val(k)) for k in
                        ("dcs", "dx0", "dx1", "dt0", "dt1", "dsx", "dss",
                         "dgp", "sv", "ss", "st") if em.val(k) > 0]

            em.replay_block(nc, sems)

        if not sge("full"):
            emx = Em(em.cnt)
            emx.op("gpsimd", (lambda eng, xb0=xb[0]: eng.dma_start(
                xb0[:, :], x_d[0:128, :])),
                waits=p1_final, inc=("dgp", 16))
            emx.op("gpsimd", (lambda eng, xb0=xb[0]: eng.dma_start(
                out_d[0:128, :], xb0[:, :])),
                waits=(("dgp", emx.val("dgp")),), inc=("dgp", 16))
            emx.op("gpsimd", lambda eng: None,
                   waits=[(k, emx.val(k)) for k in
                          ("dgp", "sv", "ss", "st", "dcs", "dx0", "dx1",
                           "dt0", "dt1", "dsx", "dss") if emx.val(k) > 0])
            emx.replay_block(nc, sems)
            return nc

        em2 = Em(em.cnt)
        lm2 = {}
        with ExitStack() as p2:
            e2 = p2.enter_context
            wfst = e2(nc.sbuf_tensor("wfst", [128, 8 * D], F32))
            wq8 = e2(nc.sbuf_tensor("wq8", [128, 8 * D], FP8))
            bias_t = e2(nc.sbuf_tensor("bias_t", [128, D], F32))
            xnt2 = [e2(nc.sbuf_tensor(f"xnt2_{i}", [128, D], F32))
                    for i in range(2)]
            xb2 = [e2(nc.sbuf_tensor(f"xb2_{i}", [128, D], F32))
                   for i in range(2)]
            sb2 = [e2(nc.sbuf_tensor(f"sb2_{i}", [128, D], BF16))
                   for i in range(2)]
            xq8 = [e2(nc.sbuf_tensor(f"xq8_{i}", [128, D], FP8))
                   for i in range(2)]
            ysb = [e2(nc.sbuf_tensor(f"ysb_{i}", [128, D], F32))
                   for i in range(2)]
            gff = [e2(nc.sbuf_tensor(f"gff_{i}", [128, D], BF16))
                   for i in range(2)]
            prod = [e2(nc.sbuf_tensor(f"prod_{i}", [128, D], F32))
                    for i in range(2)]
            outb = [e2(nc.sbuf_tensor(f"outb_{i}", [128, D], F32))
                    for i in range(2)]
            ps_y = [e2(nc.psum_tensor(f"ps_y{i}", [128, D], F32))
                    for i in range(2)]

            for s in range(8):
                em2.op("gpsimd", (lambda eng, s_=s: eng.dma_start(
                    wfst[:, s_ * D:(s_ + 1) * D],
                    wft_d[s_ * 128:(s_ + 1) * 128, :])),
                    waits=(p1_final if s == 0 else ()), inc=("dgp", 16))
            wf_ld = em2.val("dgp")

            amw_red = em2.op("vector", (lambda eng: eng.reduce_max(
                stats[:, 13:14], wfst[:, :], AX.X,
                apply_absolute_value=True)),
                waits=[("dgp", wf_ld)] + p1_final, inc=("sv", 1))
            h1 = em2.op("gpsimd", (lambda eng: eng.dma_start(
                red_scr2[:, :], stats[:, 13:14])),
                waits=(("sv", amw_red),), inc=("dgp", 16))
            h2 = em2.op("gpsimd", (lambda eng: eng.dma_start(
                rowbuf[:, :], bass.AP(red_scr2, 0, [[1, 1], [1, 128]]))),
                waits=(("dgp", h1),), inc=("dgp", 16))
            amw_sc = em2.op("vector", (lambda eng: eng.reduce_max(
                stats[0:1, 13:14], rowbuf[:, :], AX.X)),
                waits=(("dgp", h2),), inc=("sv", 1))
            h3 = em2.op("gpsimd", (lambda eng: eng.dma_start(
                wmx_scr[:, :], stats[0:1, 13:14])),
                waits=(("sv", amw_sc),), inc=("dgp", 16))
            h4 = em2.op("gpsimd", (lambda eng: eng.dma_start(
                amaxb[:, 1:2], bass.AP(wmx_scr, 0, [[0, 128], [1, 1]]))),
                waits=(("dgp", h3),), inc=("dgp", 16))
            amg_bc = em2.op("gpsimd", (lambda eng: eng.dma_start(
                stats[:, 9:10], bass.AP(cc_out, 0, [[0, 128], [1, 1]]))),
                waits=(("sq", cc_done), ("dgp", h4)), inc=("dgp", 16))
            k1 = em2.op("vector", (lambda eng: eng.tensor_scalar(
                stats[:, 9:10], stats[:, 9:10], 1e-12, None, ALU.max)),
                waits=(("dgp", amg_bc),), inc=("sv", 1))
            k2 = em2.op("vector", (lambda eng: eng.reciprocal(
                stats[:, 10:11], stats[:, 9:10])), waits=(("sv", k1),),
                inc=("sv", 1))
            k3 = em2.op("vector", (lambda eng: eng.tensor_scalar(
                stats[:, 10:11], stats[:, 10:11], 224.0, None, ALU.mult)),
                waits=(("sv", k2),), inc=("sv", 1))
            k4 = em2.op("vector", (lambda eng: eng.tensor_scalar(
                amaxb[:, 1:2], amaxb[:, 1:2], 1e-12, None, ALU.max)),
                waits=(("sv", k3),), inc=("sv", 1))
            k5 = em2.op("vector", (lambda eng: eng.reciprocal(
                stats[:, 11:12], amaxb[:, 1:2])), waits=(("sv", k4),),
                inc=("sv", 1))
            k6 = em2.op("vector", (lambda eng: eng.tensor_scalar(
                stats[:, 11:12], stats[:, 11:12], 224.0, None, ALU.mult)),
                waits=(("sv", k5),), inc=("sv", 1))
            k7 = em2.op("vector", (lambda eng: eng.tensor_tensor(
                stats[:, 12:13], stats[:, 10:11], stats[:, 11:12], ALU.mult)),
                waits=(("sv", k6),), inc=("sv", 1))
            sc_done = em2.op("vector", (lambda eng: eng.reciprocal(
                stats[:, 12:13], stats[:, 12:13])), waits=(("sv", k7),),
                inc=("sv", 1))
            wq8_done = em2.op("scalar", (lambda eng: eng.activation(
                wq8[:, :], wfst[:, :], AF.Identity, scale=stats[:, 11:12])),
                waits=[("sv", sc_done)] + p1_final, inc=("ss", 1))
            bias_bc = em2.op("gpsimd", (lambda eng: eng.dma_start(
                bias_t[:, :], bass.AP(bias_d, 0, [[0, 128], [1, D]]))),
                waits=tuple(p1_final), inc=("dgp", 16))

            for t in range(n_tiles if sge("b1") else 0):
                i = t % 2
                r0 = t * 128
                asem, bsem, psem, osem = (f"da{i}", f"db{i}", f"dp{i}",
                                          f"do{i}")
                xnt_in = em2.op("sync", (lambda eng, i_=i, r=r0: eng.dma_start(
                    xnt2[i_][:, :], xnt_scr[r:r + 128, :])),
                    waits=([("ss", lm2[("xq", t - 2)])] if t >= 2
                           else [("sv", sc_done)]),
                    inc=(asem, 16))
                x_in = em2.op("sync", (lambda eng, i_=i, r=r0: eng.dma_start(
                    xb2[i_][:, :], x_d[r:r + 128, :])),
                    waits=([("sv", lm2[("comb", t - 2)])] if t >= 2 else []),
                    inc=(bsem, 16))
                s_in = em2.op("gpsimd", (lambda eng, i_=i, r=r0: eng.dma_start(
                    sb2[i_][:, :], s_scr[r:r + 128, :])),
                    waits=([("sv", lm2[("prod", t - 2)])] if t >= 2 else []),
                    inc=(psem, 16))
                lm2[("xq", t)] = em2.op("scalar",
                                        (lambda eng, i_=i: eng.activation(
                                            xq8[i_][:, :], xnt2[i_][:, :],
                                            AF.Identity,
                                            scale=stats[:, 10:11])),
                                        waits=((asem, xnt_in),
                                               ("ss", wq8_done)),
                                        inc=("ss", 1))
                w_mm = [("ss", lm2[("xq", t)])]
                if t >= 2:
                    w_mm.append(("sv", lm2[("yev", t - 2)]))
                for s in range(8):
                    em2.op("tensor", (lambda eng, i_=i, s_=s: eng.matmul(
                        ps_y[i_][:, 0:512],
                        xq8[i_][:, s_ * 128:(s_ + 1) * 128],
                        wq8[:, s_ * D:s_ * D + 512],
                        start=(s_ == 0), stop=(s_ == 7))),
                        waits=(w_mm if s == 0 else ()))
                    em2.op("tensor", (lambda eng, i_=i, s_=s: eng.matmul(
                        ps_y[i_][:, 512:1024],
                        xq8[i_][:, s_ * 128:(s_ + 1) * 128],
                        wq8[:, s_ * D + 512:(s_ + 1) * D],
                        start=(s_ == 0), stop=(s_ == 7))),
                        inc=("st", 1) if s == 7 else None)
                y_mm = em2.val("st")
                lm2[("yev", t)] = em2.op(
                    "vector", (lambda eng, i_=i: eng.scalar_tensor_tensor(
                        ysb[i_][:, :], ps_y[i_][:, :], stats[:, 12:13],
                        bias_t[:, :], ALU.mult, ALU.add)),
                    waits=(("st", y_mm), ("dgp", bias_bc)), inc=("sv", 1))
                gff_done = em2.op("scalar", (lambda eng, i_=i: eng.activation(
                    gff[i_][:, :], ysb[i_][:, :], AF.Silu)),
                    waits=(("sv", lm2[("yev", t)]),), inc=("ss", 1))
                lm2[("prod", t)] = em2.op(
                    "vector", (lambda eng, i_=i: eng.tensor_tensor(
                        prod[i_][:, :], gff[i_][:, :], sb2[i_][:, :],
                        ALU.mult)),
                    waits=(("ss", gff_done), (psem, s_in)), inc=("sv", 1))
                lm2[("comb", t)] = em2.op(
                    "vector", (lambda eng, i_=i: eng.tensor_tensor(
                        outb[i_][:, :], xb2[i_][:, :], prod[i_][:, :],
                        ALU.add)),
                    waits=((bsem, x_in), ("sv", lm2[("prod", t)]),
                           ((osem, lm2[("ost", t - 2)]) if t >= 2
                            else ("sv", 0))),
                    inc=("sv", 1))
                lm2[("ost", t)] = em2.op(
                    "scalar", (lambda eng, i_=i, r=r0: eng.dma_start(
                        out_d[r:r + 128, :], outb[i_][:, :])),
                    waits=(("sv", lm2[("comb", t)]),), inc=(osem, 16))

            em2.op("scalar", lambda eng: None,
                   waits=(("do0", em2.val("do0")), ("do1", em2.val("do1"))))
            sv_f = em2.op("vector", lambda eng: eng.engine_nop(),
                          inc=("sv", 1))
            ss_f = em2.op("scalar", lambda eng: eng.copy(
                stats[0:1, 0:1], stats[0:1, 0:1]), inc=("ss", 1))
            em2.op("gpsimd", lambda eng: None,
                   waits=[("sv", sv_f), ("ss", ss_f), ("st", em2.val("st"))]
                   + [(k, em2.val(k)) for k in
                      ("da0", "da1", "db0", "db1", "dp0", "dp1",
                       "do0", "do1", "dgp") if em2.val(k) > 0])
            em2.replay_block(nc, sems)

    return nc


def _prep_inputs(Wq, Wk, Wv, Wg, Wo, gnw, Wf, bias, n_tiles):
    def perm_qk(W):
        Wp = W.reshape(H, 2, 128, D).transpose(1, 0, 2, 3).reshape(D, D)
        return np.ascontiguousarray(Wp.T)
    return {
        "wqt": perm_qk(np.asarray(Wq)),
        "wkt": perm_qk(np.asarray(Wk)),
        "wvt": np.ascontiguousarray(np.asarray(Wv).T),
        "wgt": np.ascontiguousarray(np.asarray(Wg).T),
        "wot": np.ascontiguousarray(np.asarray(Wo).T),
        "wft": np.ascontiguousarray(np.asarray(Wf).T),
        "fbias": np.asarray(bias).reshape(1, D).astype(np.float32),
        "gnw": np.ascontiguousarray(
            np.asarray(gnw).reshape(4, 128).T.astype(np.float32)),
        "idf": np.eye(128, dtype=np.float32),
        "mskt": np.triu(np.ones((128, 128), np.float32)),
        "tabs": host_tables(n_tiles * C),
    }


def kernel(L0_attn_x, L0_attn_Wq, L0_attn_Wk, L0_attn_Wv, L0_attn_Wg,
           L0_attn_Wo, L0_attn_gnorm_w, L0_ffn_weight, L0_ffn_bias):
    n_tiles = 64
    n_cores = 8
    x = np.asarray(L0_attn_x)
    base = _prep_inputs(L0_attn_Wq, L0_attn_Wk, L0_attn_Wv, L0_attn_Wg,
                        L0_attn_Wo, L0_attn_gnorm_w, L0_ffn_weight,
                        L0_ffn_bias, n_tiles)
    nc = build(n_tiles, n_cores)
    in_maps = [dict(base, x=np.ascontiguousarray(x[b]))
               for b in range(n_cores)]
    res = run_bass_kernel_spmd(nc, in_maps, core_ids=list(range(n_cores)))
    out = np.stack([res.results[b]["out"] for b in range(n_cores)], axis=0)
    return out.astype(np.float32)


# version 07
# speedup vs baseline: 1.0052x; 1.0052x over previous
import numpy as np
import ml_dtypes
from contextlib import ExitStack

import concourse.bass as bass
import concourse.mybir as mybir
from concourse.bass_utils import run_bass_kernel_spmd

F32 = mybir.dt.float32
BF16 = mybir.dt.bfloat16
FP8 = mybir.dt.float8e4
AF = mybir.ActivationFunctionType
ALU = mybir.AluOpType
AX = mybir.AxisListType

D = 1024
H = 4
HV = 512
C = 128
EPS = 1e-5

ENGS = ("sync", "gpsimd", "vector", "scalar", "tensor")


class Em:

    def __init__(self, cnt=None):
        self.cmds = {e: [] for e in ENGS}
        self.cnt = dict(cnt) if cnt else {}

    def val(self, sem):
        return self.cnt.get(sem, 0)

    def op(self, eng, build_fn, waits=(), inc=None):
        waits = tuple(waits)

        def cmd(engine, sems):
            for sname, v in waits:
                if v > 0:
                    engine.wait_ge(sems[sname], v)
            instr = build_fn(engine)
            if inc is not None and instr is not None:
                instr.then_inc(sems[inc[0]], inc[1])
        self.cmds[eng].append(cmd)
        if inc is not None:
            self.cnt[inc[0]] = self.cnt.get(inc[0], 0) + inc[1]
            return self.cnt[inc[0]]
        return None

    def replay_block(self, nc, sems):
        with nc.Block() as block:
            for name in ENGS:
                def mk(n):
                    def run(engine):
                        for cmd in self.cmds[n]:
                            cmd(engine, sems)
                    return run
                getattr(block, name)(mk(name))


def _gammas():
    return [np.float32(1.0) - np.float32(2.0 ** (-5.0 - h)) for h in range(H)]


def host_tables(T):
    j = np.arange(0, 256, 2, dtype=np.float32) / np.float32(256)
    inv = 1.0 / (10000.0 ** j.astype(np.float64))
    freqs = (np.arange(T, dtype=np.float32)[:, None]
             * inv.astype(np.float32)[None, :]).astype(np.float32)
    cos64 = np.cos(freqs.astype(np.float64))
    sin64 = np.sin(freqs.astype(np.float64))
    i_loc = (np.arange(T) % C).astype(np.float64)
    blocks = []
    for kind in range(4):
        for h in range(H):
            g = np.float64(_gammas()[h])
            if kind < 2:
                scale = (g ** i_loc) / 16.0
                base = cos64 if kind == 0 else sin64
            else:
                scale = g ** (-i_loc)
                base = cos64 if kind == 2 else sin64
            blocks.append((base * scale[:, None]).astype(np.float32))
    return np.ascontiguousarray(np.concatenate(blocks, axis=1))


_STAGES = ("w", "b1", "b2", "b3", "b4", "ba", "bb", "bc", "base", "proj", "vg", "tq", "ret", "wo", "cc", "full")


def build(n_tiles, n_cores, stage="full", dbg=False):
    sidx = _STAGES.index(stage)

    def sge(name):
        return sidx >= _STAGES.index(name)

    T = n_tiles * C
    nc = bass.Bass("TRN2", target_bir_lowering=False,
                   detect_race_conditions=False)

    x_d = nc.dram_tensor("x", [T, D], F32, kind="ExternalInput")
    wqt_d = nc.dram_tensor("wqt", [D, D], F32, kind="ExternalInput")
    wkt_d = nc.dram_tensor("wkt", [D, D], F32, kind="ExternalInput")
    wvt_d = nc.dram_tensor("wvt", [D, 2 * D], F32, kind="ExternalInput")
    wgt_d = nc.dram_tensor("wgt", [D, 2 * D], F32, kind="ExternalInput")
    wot_d = nc.dram_tensor("wot", [2 * D, D], F32, kind="ExternalInput")
    wft_d = nc.dram_tensor("wft", [D, D], F32, kind="ExternalInput")
    bias_d = nc.dram_tensor("fbias", [1, D], F32, kind="ExternalInput")
    gnw_d = nc.dram_tensor("gnw", [128, 4], F32, kind="ExternalInput")
    idf_d = nc.dram_tensor("idf", [128, 128], F32, kind="ExternalInput")
    mskt_d = nc.dram_tensor("mskt", [128, 128], F32, kind="ExternalInput")
    tabs_d = nc.dram_tensor("tabs", [T, 2048], F32, kind="ExternalInput")

    out_d = nc.dram_tensor("out", [T, D], F32, kind="ExternalOutput")

    scr_kind = "ExternalOutput" if dbg else "Internal"
    xnt_scr = nc.dram_tensor("xnt_scr", [T, D], F32, kind=scr_kind)
    s_scr = nc.dram_tensor("s_scr", [T, D], BF16, kind=scr_kind)
    if dbg:
        xh_scr = nc.dram_tensor("xh_scr", [T, D], BF16, kind="ExternalOutput")
        xl_scr = nc.dram_tensor("xl_scr", [T, D], BF16, kind="ExternalOutput")
    cc_in = nc.dram_tensor("cc_in", [1, 1], F32)
    cc_out = nc.dram_tensor("cc_out", [1, 1], F32)
    red_scr = nc.dram_tensor("red_scr", [128, 1], F32)
    red_scr2 = nc.dram_tensor("red_scr2", [128, 1], F32)
    wmx_scr = nc.dram_tensor("wmx_scr", [1, 1], F32)

    gams = _gammas()
    gCs = [float(np.float32(np.float64(g) ** C)) for g in gams]

    with ExitStack() as top:
        e = top.enter_context
        idb = e(nc.sbuf_tensor("idb_sb", [128, 128], BF16))
        mskt = e(nc.sbuf_tensor("mskt_sb", [128, 128], BF16))
        gnw = e(nc.sbuf_tensor("gnw_sb", [128, 4], F32))
        amaxb = e(nc.sbuf_tensor("amaxb", [128, max(n_tiles, 2)], F32))
        stats = e(nc.sbuf_tensor("stats", [128, 16], F32))
        gvar = e(nc.sbuf_tensor("gvar", [128, 4], F32))
        gv4 = e(nc.sbuf_tensor("gv4", [128, 4], F32))
        rtg = e(nc.sbuf_tensor("rtg", [128, 4], F32))
        rsg = e(nc.sbuf_tensor("rsg", [128, 4], F32))
        rowbuf = e(nc.sbuf_tensor("rowbuf", [1, 128], F32))

        sems = {}
        for s in ("dxa", "dcs", "dx0", "dx1", "dt0", "dt1", "dsx", "dss",
                  "dsc", "dgp", "da0", "da1", "db0", "db1", "dp0", "dp1",
                  "do0", "do1", "sv", "ss", "st", "sq"):
            sems[s] = e(nc.semaphore(s))

        em = Em()
        lm = {}

        SEMR = range(155, 256)
        em.op("gpsimd", lambda eng: eng.sem_clear(SEMR))
        em.op("gpsimd", (lambda eng: eng.dma_start(
            rowbuf[0:1, 0:1], idf_d[0:1, 0:1])), inc=("dgp", 16))
        for _eng in ("sync", "vector", "scalar", "tensor"):
            em.op(_eng, lambda eng: None, waits=(("dgp", 16),))

        def lw(sem, key):
            return [(sem, lm[key])] if key in lm else []

        with ExitStack() as p1:
            e1 = p1.enter_context
            wq = e1(nc.sbuf_tensor("wq_sb", [128, 8 * D], BF16))
            wk = e1(nc.sbuf_tensor("wk_sb", [128, 8 * D], BF16))
            wv = e1(nc.sbuf_tensor("wv_sb", [128, 16 * D], BF16))
            wg = e1(nc.sbuf_tensor("wg_sb", [128, 16 * D], BF16))
            wo = e1(nc.sbuf_tensor("wo_sb", [128, 16 * D], BF16))
            Tst = e1(nc.sbuf_tensor("Tst", [128, 4096], F32))
            xb = [e1(nc.sbuf_tensor(f"xb{i}", [128, D], F32)) for i in range(2)]
            tb = [e1(nc.sbuf_tensor("tb0", [128, 2048], F32))]
            xn = e1(nc.sbuf_tensor("xn", [128, D], F32))
            xntb = e1(nc.sbuf_tensor("xntb", [128, D], BF16))
            xh = e1(nc.sbuf_tensor("xh", [128, D], BF16))
            qp = e1(nc.sbuf_tensor("qp", [128, D], BF16))
            kp = e1(nc.sbuf_tensor("kp", [128, D], BF16))
            qpT = e1(nc.sbuf_tensor("qpT", [128, D], BF16))
            kpT = e1(nc.sbuf_tensor("kpT", [128, D], BF16))
            vb = e1(nc.sbuf_tensor("vb", [128, 2 * D], BF16))
            mb = e1(nc.sbuf_tensor("mb", [128, 2 * D], BF16))
            Sbf = e1(nc.sbuf_tensor("Sbf", [128, 1024], BF16))
            atb = e1(nc.sbuf_tensor("atb", [128, 128], BF16))
            obf = e1(nc.sbuf_tensor("obf", [128, 2 * D], BF16))
            ogb = e1(nc.sbuf_tensor("ogb", [128, 2 * D], BF16))
            ogT = e1(nc.sbuf_tensor("ogT", [128, 2 * D], BF16))
            sbo = e1(nc.sbuf_tensor("sbo", [128, D], BF16))
            tmp1 = e1(nc.sbuf_tensor("tmp1", [128, 512], BF16))
            tmp2 = e1(nc.sbuf_tensor("tmp2", [128, 512], BF16))
            junk = e1(nc.sbuf_tensor("junk", [128, D], BF16))
            junk2 = e1(nc.sbuf_tensor("junk2", [128, HV], BF16))

            pA = e1(nc.psum_tensor("pA", [128, D], F32))
            pB = e1(nc.psum_tensor("pB", [128, D], F32))
            pC = e1(nc.psum_tensor("pC", [128, D], F32))
            pD = e1(nc.psum_tensor("pD", [128, D], F32))

            def wload(dst, src_d, nslab, ncol):
                for s in range(nslab):
                    em.op("gpsimd",
                          (lambda eng, d=dst, sd=src_d, s_=s, nco=ncol:
                           eng.dma_start(d[:, s_ * nco:(s_ + 1) * nco],
                                         sd[s_ * 128:(s_ + 1) * 128, :])),
                          inc=("dgp", 16))
            wload(wq, wqt_d, 8, D)
            wload(wk, wkt_d, 8, D)
            wload(wv, wvt_d, 8, 2 * D)
            wload(wg, wgt_d, 8, 2 * D)
            wload(wo, wot_d, 16, D)
            em.op("gpsimd", lambda eng: eng.dma_start(idb[:, :], idf_d[:, :]),
                  inc=("dgp", 16))
            em.op("gpsimd", lambda eng: eng.dma_start(mskt[:, :], mskt_d[:, :]),
                  inc=("dgp", 16))
            w_loaded = em.val("dgp")
            em.op("sync", lambda eng: eng.dma_start(gnw[:, :], gnw_d[:, :]),
                  inc=("dcs", 16))
            const_xa = em.val("dcs")

            gnw_done = None
            for s in range(16):
                gnw_done = em.op(
                    "vector",
                    (lambda eng, s_=s: eng.tensor_scalar(
                        wo[:, s_ * D:(s_ + 1) * D], wo[:, s_ * D:(s_ + 1) * D],
                        gnw[:, (s_ % 4):(s_ % 4) + 1], None, ALU.mult)),
                    waits=((("dgp", w_loaded), ("dcs", const_xa))
                           if s == 0 else ()),
                    inc=("sv", 1))
            tst_init = em.op("vector", lambda eng: eng.memset(Tst[:, :], 0.0),
                             inc=("sv", 1))

            def emit_ln(t):
                xcur = xb[t % 2]
                r0 = t * 128
                w_x = lw("ss", ("xfree", t - 2))
                xsem = f"dx{t % 2}"
                xin = em.op("sync",
                            (lambda eng, xc=xcur, r=r0:
                             eng.dma_start(xc[:, :], x_d[r:r + 128, :])),
                            waits=w_x, inc=(xsem, 16))

                sum_done = em.op("vector",
                                 (lambda eng, xc=xcur: eng.reduce_sum(
                                     stats[:, 0:1], xc[:, :], AX.X)),
                                 waits=((xsem, xin),), inc=("sv", 1))
                sq_done = em.op("scalar",
                                (lambda eng, xc=xcur: eng.activation(
                                    junk[:, :], xc[:, :], AF.Square,
                                    accum_out=stats[:, 1:2])),
                                waits=[(xsem, xin)]
                                + lw("st", ("xnt_mm", t - 1)),
                                inc=("ss", 1))
                c1 = em.op("vector", lambda eng: eng.tensor_scalar(
                    stats[:, 2:3], stats[:, 0:1], 1.0 / D, None, ALU.mult),
                    waits=(("sv", sum_done),), inc=("sv", 1))
                c2 = em.op("vector", lambda eng: eng.tensor_tensor(
                    stats[:, 13:14], stats[:, 2:3], stats[:, 2:3], ALU.mult),
                    waits=(("sv", c1),), inc=("sv", 1))
                c3 = em.op("vector", lambda eng: eng.scalar_tensor_tensor(
                    stats[:, 3:4], stats[:, 1:2], 1.0 / D, stats[:, 13:14],
                    ALU.mult, ALU.subtract),
                    waits=(("ss", sq_done), ("sv", c2)), inc=("sv", 1))
                var_done = em.op("vector", lambda eng: eng.tensor_scalar(
                    stats[:, 3:4], stats[:, 3:4], EPS, None, ALU.add),
                    waits=(("sv", c3),), inc=("sv", 1))
                rtv_done = em.op("scalar", lambda eng: eng.activation(
                    stats[:, 4:5], stats[:, 3:4], AF.Sqrt),
                    waits=(("sv", var_done),), inc=("ss", 1))
                c5 = em.op("vector", lambda eng: eng.reciprocal(
                    stats[:, 5:6], stats[:, 4:5]), waits=(("ss", rtv_done),),
                    inc=("sv", 1))
                nmr_done = em.op("vector", lambda eng: eng.scalar_tensor_tensor(
                    stats[:, 6:7], stats[:, 2:3], -1.0, stats[:, 5:6],
                    ALU.mult, ALU.mult), waits=(("sv", c5),), inc=("sv", 1))
                w_xn = ([("sv", nmr_done)] + lw("st", ("xnt_mm", t - 1))
                        + lw("dsx", ("xnts", t - 1))
                        + lw("sv", ("xntf_ev", t - 1)))
                xn_done = em.op("scalar", (lambda eng, xc=xcur: eng.activation(
                    xn[:, :], xc[:, :], AF.Identity,
                    bias=stats[:, 6:7], scale=stats[:, 5:6])),
                    waits=w_xn, inc=("ss", 1))
                lm[("xfree", t)] = xn_done
                em.op("vector", (lambda eng, t_=t: eng.reduce_max(
                    amaxb[:, t_:t_ + 1], xn[:, :], AX.X,
                    apply_absolute_value=True)),
                    waits=(("ss", xn_done),), inc=("sv", 1))

                hl_done = em.op("vector", lambda eng: eng.tensor_copy(
                    xh[:, :], xn[:, :]), waits=(("ss", xn_done),))
                hl_done = em.op("vector", lambda eng: eng.tensor_tensor(
                    junk[:, :], xn[:, :], xh[:, :], ALU.subtract),
                    inc=("sv", 1))
                lm[("hl", t)] = hl_done

            def emit_ogT_s(t, last=False):
                r0 = t * 128
                w_ogt = [("sv", lm[("og_done", t)]),
                         ("sv", lm[("tupd", t, H - 1)])]
                if last:
                    w_ogt += [("ss", lm[("o_ev", t, H - 1)]),
                              ("sv", lm[("at_ev", t, H - 1)])]
                else:
                    w_ogt += [("ss", lm[("xnt_ev", t + 1)]),
                              ("sv", lm[("xntf_ev", t + 1)])]
                for s_ in range(16):
                    dst = pA if s_ < 8 else pD
                    em.op("tensor", (lambda eng, s2=s_, d=dst: eng.matmul(
                        d[:, (s2 % 8) * 128:(s2 % 8) * 128 + 128],
                        ogb[:, s2 * 128:(s2 + 1) * 128], idb[:, :],
                        start=True, stop=True)),
                        waits=(w_ogt if s_ == 0 else ()),
                        inc=("st", 1) if s_ == 15 else None)
                lm[("ogt_mm", t)] = em.val("st")
                em.op("scalar", lambda eng: eng.copy(ogT[:, 0:1024], pA[:, :]),
                      waits=(("st", lm[("ogt_mm", t)]),))
                ogt_ev = em.op("scalar", lambda eng: eng.copy(
                    ogT[:, 1024:2048], pD[:, :]), inc=("ss", 1))
                lm[("ogt_ev", t)] = ogt_ev
                w_s = [("ss", ogt_ev)] + lw("dss", ("s_store", t - 1))
                for s_ in range(16):
                    em.op("tensor", (lambda eng, s2=s_: eng.matmul(
                        pD[:, 0:512], ogT[:, s2 * 128:(s2 + 1) * 128],
                        wo[:, s2 * D:s2 * D + 512],
                        start=(s2 == 0), stop=(s2 == 15))),
                        waits=(w_s if s_ == 0 else ()))
                    em.op("tensor", (lambda eng, s2=s_: eng.matmul(
                        pD[:, 512:1024], ogT[:, s2 * 128:(s2 + 1) * 128],
                        wo[:, s2 * D + 512:(s2 + 1) * D],
                        start=(s2 == 0), stop=(s2 == 15))),
                        inc=("st", 1) if s_ == 15 else None)
                lm[("s_mm", t)] = em.val("st")
                lm[("s_ev", t)] = em.op(
                    "vector", (lambda eng: eng.tensor_copy(
                        sbo[:, :], pD[:, :])),
                    waits=(("st", lm[("s_mm", t)]),), inc=("sv", 1))
                lm[("s_store", t)] = em.op(
                    "scalar", (lambda eng, r=r0: eng.dma_start(
                        s_scr[r:r + 128, :], sbo[:, :])),
                    waits=(("sv", lm[("s_ev", t)]),), inc=("dss", 16))


            if sge("b1"):
                emit_ln(0)

            for t in range(n_tiles if sge("b1") else 0):
                tcur = tb[0]
                r0 = t * 128

                w_t = lw("sv", ("tfree", t - 1))
                tsem = f"dt{t % 2}"
                tin = em.op("sync",
                            (lambda eng, tc=tcur, r=r0:
                             eng.dma_start(tc[:, :], tabs_d[r:r + 128, :])),
                            waits=w_t, inc=(tsem, 16))

                if not sge("bb"):
                    continue
                hl_done = lm[("hl", t)]
                if dbg:
                    em.op("gpsimd", (lambda eng, r=r0: eng.dma_start(
                        xh_scr[r:r + 128, :], xh[:, :])),
                        waits=(("sv", hl_done),), inc=("dgp", 16))
                    em.op("gpsimd", (lambda eng, r=r0: eng.dma_start(
                        xl_scr[r:r + 128, :], junk[:, :])),
                        waits=(("sv", hl_done),), inc=("dgp", 16))
                w_tr = ([("sv", hl_done), ("dgp", w_loaded)]
                        + lw("ss", ("o_ev", t - 1, H - 1))
                        + lw("sv", ("at_ev", t - 1, H - 1))
                        + lw("ss", ("xnt_ev", t - 1))
                        + lw("sv", ("xntf_ev", t - 1)))
                for s in range(8):
                    em.op("tensor", (lambda eng, s_=s: eng.matmul(
                        pA[:, s_ * 128:(s_ + 1) * 128],
                        xh[:, s_ * 128:(s_ + 1) * 128], idb[:, :],
                        start=True, stop=False)),
                        waits=(w_tr if s == 0 else ()))
                    em.op("tensor", (lambda eng, s_=s: eng.matmul(
                        pA[:, s_ * 128:(s_ + 1) * 128],
                        junk[:, s_ * 128:(s_ + 1) * 128], idb[:, :],
                        start=False, stop=True)),
                        inc=("st", 1) if s == 7 else None)
                lm[("xnt_mm", t)] = em.val("st")
                if not sge("bc"):
                    continue
                w_evf = ([("st", lm[("xnt_mm", t)])]
                         + lw("dsx", ("xnts", t - 1)))
                xntf_ev = em.op("vector", lambda eng: eng.tensor_copy(
                    xn[:, :], pA[:, :]), waits=w_evf, inc=("sv", 1))
                lm[("xntf_ev", t)] = xntf_ev
                w_ev = ([("st", lm[("xnt_mm", t)]), ("sv", xntf_ev)]
                        + lw("st", ("g_mm", t - 1)))
                xnt_ev = em.op("scalar", lambda eng: eng.copy(
                    xntb[:, :], pA[:, :]), waits=w_ev, inc=("ss", 1))
                lm[("xnt_ev", t)] = xnt_ev
                if not sge("base"):
                    continue
                lm[("xnts", t)] = em.op("scalar",
                                        (lambda eng, r=r0: eng.dma_start(
                                            xnt_scr[r:r + 128, :],
                                            xn[:, :])),
                                        waits=(("sv", xntf_ev),),
                                        inc=("dsx", 16))

                if not sge("proj"):
                    continue
                def proj_round(wsrc, col0, name_mm, waits_first,
                               dlo=None, dhi=None):
                    dlo = dlo if dlo is not None else pB
                    dhi = dhi if dhi is not None else pC
                    for s in range(8):
                        base = col0 + s * (wsrc.shape[1] // 8)
                        for blk in range(4):
                            dst = (dlo if blk < 2 else dhi)
                            dcol = (blk % 2) * 512
                            em.op("tensor",
                                  (lambda eng, s_=s, b_=base, bl=blk, d=dst,
                                   dc=dcol, w_=wsrc: eng.matmul(
                                       d[:, dc:dc + 512],
                                       xntb[:, s_ * 128:(s_ + 1) * 128],
                                       w_[:, b_ + bl * 512:b_ + bl * 512 + 512],
                                       start=(s_ == 0), stop=(s_ == 7))),
                                  waits=(waits_first if (s == 0 and blk == 0)
                                         else ()),
                                  inc=(("st", 1) if (s == 7 and blk == 3)
                                       else None))
                    return em.val("st")

                w_qk = [("ss", xnt_ev)]
                if t == 0:
                    w_qk += [("dgp", w_loaded), ("sv", gnw_done)]
                else:
                    w_qk += lw("sv", ("s_ev", t - 1))
                for s in range(8):
                    for blk in range(4):
                        dst = pB if blk < 2 else pC
                        wsrc = wq if blk < 2 else wk
                        dcol = (blk % 2) * 512
                        scol = (blk % 2) * 512
                        em.op("tensor",
                              (lambda eng, s_=s, bl=blk, d=dst, dc=dcol,
                               w_=wsrc, sc=scol: eng.matmul(
                                   d[:, dc:dc + 512],
                                   xntb[:, s_ * 128:(s_ + 1) * 128],
                                   w_[:, s_ * D + sc:s_ * D + sc + 512],
                                   start=(s_ == 0), stop=(s_ == 7))),
                              waits=(w_qk if (s == 0 and blk == 0) else ()),
                              inc=(("st", 1) if (s == 7 and blk == 3)
                                   else None))
                qk_mm = em.val("st")

                if t >= 1:
                    emit_ogT_s(t - 1)

                first_w = [("st", qk_mm), (tsem, tin), ("dcs", const_xa)]
                seq = [
                    (pB, qp, 0, 512, 0, 512, ALU.subtract, 0),
                    (pB, qp, 512, 0, 0, 512, ALU.add, 512),
                    (pC, kp, 0, 512, 1024, 1536, ALU.subtract, 0),
                    (pC, kp, 512, 0, 1024, 1536, ALU.add, 512),
                ]
                n_r = 0
                for (ps, dst, ac, bc, ct, st_, alu, oc) in seq:
                    em.op("vector", (lambda eng, p_=ps, a_=ac, c_=ct:
                                     eng.tensor_tensor(
                                         tmp1[:, :], p_[:, a_:a_ + 512],
                                         tcur[:, c_:c_ + 512], ALU.mult)),
                          waits=(first_w if n_r == 0 else ()))
                    n_r += 1
                    em.op("vector", (lambda eng, p_=ps, b_=bc, s2=st_:
                                     eng.tensor_tensor(
                                         tmp2[:, :], p_[:, b_:b_ + 512],
                                         tcur[:, s2:s2 + 512], ALU.mult)))
                    em.op("vector", (lambda eng, d=dst, o_=oc, a2=alu:
                                     eng.tensor_tensor(
                                         d[:, o_:o_ + 512], tmp1[:, :],
                                         tmp2[:, :], a2)),
                          inc=("sv", 1) if n_r == 4 else None)
                rope_done = em.val("sv")
                lm[("rope_done", t)] = rope_done
                lm[("tfree", t)] = rope_done

                if not sge("vg"):
                    continue
                w_v = ([("ss", xnt_ev), ("sv", xntf_ev)]
                       + lw("sv", ("s_ev", t - 1)))
                v_mm = proj_round(wv, 0, "v", w_v, dlo=pA, dhi=pD)
                w_vev = [("st", v_mm)] + lw("sv", ("og_done", t - 1))
                em.op("scalar", lambda eng: eng.copy(vb[:, 0:1024], pA[:, :]),
                      waits=w_vev)
                v_ev = em.op("scalar", lambda eng: eng.copy(
                    vb[:, 1024:2048], pD[:, :]), inc=("ss", 1))
                g_mm = proj_round(wg, 0, "g", [("ss", v_ev)], dlo=pA, dhi=pD)
                lm[("g_mm", t)] = g_mm
                w_m = [("st", g_mm)] + lw("sv", ("og_done", t - 1))
                em.op("scalar", lambda eng: eng.activation(
                    mb[:, 0:1024], pA[:, :], AF.Silu), waits=w_m)
                m_done = em.op("scalar", lambda eng: eng.activation(
                    mb[:, 1024:2048], pD[:, :], AF.Silu), inc=("ss", 1))
                lm[("m_done", t)] = m_done

                if not sge("tq"):
                    continue
                w_tq = [("sv", rope_done), ("dgp", w_loaded)]
                for s in range(8):
                    em.op("tensor", (lambda eng, s_=s: eng.matmul(
                        pB[:, s_ * 128:(s_ + 1) * 128],
                        qp[:, s_ * 128:(s_ + 1) * 128], idb[:, :],
                        start=True, stop=True)),
                        waits=(w_tq if s == 0 else ()))
                for s in range(8):
                    em.op("tensor", (lambda eng, s_=s: eng.matmul(
                        pC[:, s_ * 128:(s_ + 1) * 128],
                        kp[:, s_ * 128:(s_ + 1) * 128], idb[:, :],
                        start=True, stop=True)),
                        inc=("st", 1) if s == 7 else None)
                tqk_mm = em.val("st")
                em.op("scalar", lambda eng: eng.copy(qpT[:, :], pB[:, :]),
                      waits=(("st", tqk_mm),))
                qkT_ev = em.op("scalar", lambda eng: eng.copy(
                    kpT[:, :], pC[:, :]), inc=("ss", 1))
                lm[("qkT_ev", t)] = qkT_ev

                em.op("vector", lambda eng: eng.memset(gvar[:, :], 0.0),
                      waits=(("ss", qkT_ev),))

                if not sge("ret"):
                    continue
                for h in range(H):
                    gC = gCs[h]
                    w_P = [("ss", qkT_ev), ("ss", m_done)]
                    if h > 0:
                        w_P.append(("sv", lm[("tupd", t, h - 1)]))
                    em.op("tensor", (lambda eng, h_=h: eng.matmul(
                        pD[:, 0:512], kp[:, 128 * h_:128 * h_ + 128],
                        vb[:, 512 * h_:512 * h_ + 512],
                        start=True, stop=True)), waits=w_P)
                    em.op("tensor", (lambda eng, h_=h: eng.matmul(
                        pD[:, 512:1024],
                        kp[:, 512 + 128 * h_:512 + 128 * h_ + 128],
                        vb[:, 512 * h_:512 * h_ + 512],
                        start=True, stop=True)), inc=("st", 1))
                    P_mm = em.val("st")
                    w_at = []
                    if h > 0:
                        w_at.append(("sv", lm[("at_ev", t, h - 1)]))
                    else:
                        w_at += lw("ss", ("ogt_ev", t - 1))
                    em.op("tensor", (lambda eng, h_=h: eng.matmul(
                        pA[:, 0:128], kpT[:, 128 * h_:128 * h_ + 128],
                        qpT[:, 128 * h_:128 * h_ + 128],
                        start=True, stop=False)), waits=w_at)
                    em.op("tensor", (lambda eng, h_=h: eng.matmul(
                        pA[:, 0:128], kpT[:, 128 * (4 + h_):128 * (4 + h_) + 128],
                        qpT[:, 128 * (4 + h_):128 * (4 + h_) + 128],
                        start=False, stop=True)), inc=("st", 1))
                    at_mm = em.val("st")
                    w_sb = [("sv", tst_init)] if (t, h) == (0, 0) else \
                        [("st", lm[("o_mm", t, h - 1)] if h > 0
                          else lm[("o_mm", t - 1, H - 1)])]
                    em.op("scalar", (lambda eng, h_=h, g_=gC: eng.activation(
                        Sbf[:, 0:512], Tst[:, 1024 * h_:1024 * h_ + 512],
                        AF.Identity, scale=float(g_))), waits=w_sb)
                    sbf_done = em.op("scalar",
                                     (lambda eng, h_=h, g_=gC: eng.activation(
                                         Sbf[:, 512:1024],
                                         Tst[:, 1024 * h_ + 512:1024 * (h_ + 1)],
                                         AF.Identity, scale=float(g_))),
                                     inc=("ss", 1))
                    em.op("vector", (lambda eng, h_=h, g_=gC:
                                     eng.scalar_tensor_tensor(
                                         Tst[:, 1024 * h_:1024 * h_ + 512],
                                         Tst[:, 1024 * h_:1024 * h_ + 512],
                                         g_, pD[:, 0:512],
                                         ALU.mult, ALU.add)),
                          waits=(("st", P_mm), ("ss", sbf_done)))
                    lm[("tupd", t, h)] = em.op(
                        "vector", (lambda eng, h_=h, g_=gC:
                                   eng.scalar_tensor_tensor(
                                       Tst[:, 1024 * h_ + 512:1024 * (h_ + 1)],
                                       Tst[:, 1024 * h_ + 512:1024 * (h_ + 1)],
                                       g_, pD[:, 512:1024],
                                       ALU.mult, ALU.add)), inc=("sv", 1))
                    lm[("at_ev", t, h)] = em.op(
                        "vector", (lambda eng: eng.tensor_tensor(
                            atb[:, :], pA[:, 0:128], mskt[:, :], ALU.mult)),
                        waits=(("st", at_mm),), inc=("sv", 1))
                    w_o = [("ss", sbf_done), ("sv", lm[("at_ev", t, h)])]
                    if h > 0:
                        w_o.append(("ss", lm[("o_ev", t, h - 1)]))
                    em.op("tensor", (lambda eng, h_=h: eng.matmul(
                        pA[:, 512:1024], qpT[:, 128 * h_:128 * h_ + 128],
                        Sbf[:, 0:512], start=True, stop=False)), waits=w_o)
                    em.op("tensor", (lambda eng, h_=h: eng.matmul(
                        pA[:, 512:1024],
                        qpT[:, 128 * (4 + h_):128 * (4 + h_) + 128],
                        Sbf[:, 512:1024], start=False, stop=False)))
                    em.op("tensor", (lambda eng, h_=h: eng.matmul(
                        pA[:, 512:1024], atb[:, :],
                        vb[:, 512 * h_:512 * h_ + 512],
                        start=False, stop=True)), inc=("st", 1))
                    lm[("o_mm", t, h)] = em.val("st")
                    em.op("scalar", (lambda eng, h_=h: eng.activation(
                        junk2[:, :], pA[:, 512:1024], AF.Square,
                        accum_out=gvar[:, h_:h_ + 1])),
                        waits=(("st", lm[("o_mm", t, h)]),))
                    w_oev = (lw("sv", ("og_done", t - 1))
                             if h == 0 else [])
                    lm[("o_ev", t, h)] = em.op(
                        "scalar", (lambda eng, h_=h: eng.copy(
                            obf[:, 512 * h_:512 * h_ + 512],
                            pA[:, 512:1024])), waits=w_oev, inc=("ss", 1))
                lm[("qkT_free", t)] = lm[("at_ev", t, H - 1)]

                if not sge("wo"):
                    continue
                gv_done = em.op("vector", lambda eng: eng.tensor_scalar(
                    gv4[:, :], gvar[:, :], 1.0 / HV, EPS, ALU.mult, ALU.add),
                    waits=(("ss", lm[("o_ev", t, H - 1)]),), inc=("sv", 1))
                rtg_done = em.op("scalar", lambda eng: eng.activation(
                    rtg[:, :], gv4[:, :], AF.Sqrt),
                    waits=(("sv", gv_done),), inc=("ss", 1))
                rsg_done = em.op("vector", lambda eng: eng.reciprocal(
                    rsg[:, :], rtg[:, :]), waits=(("ss", rtg_done),),
                    inc=("sv", 1))
                if t + 1 < n_tiles:
                    emit_ln(t + 1)
                og_done = None
                for h in range(H):
                    og_done = em.op(
                        "vector", (lambda eng, h_=h: eng.scalar_tensor_tensor(
                            ogb[:, 512 * h_:512 * h_ + 512],
                            obf[:, 512 * h_:512 * h_ + 512],
                            rsg[:, h_:h_ + 1],
                            mb[:, 512 * h_:512 * h_ + 512],
                            ALU.mult, ALU.mult)),
                        waits=((("sv", rsg_done), ("ss", m_done))
                               if h == 0 else ()),
                        inc=("sv", 1) if h == H - 1 else None)
                lm[("og_done", t)] = og_done


            if n_tiles >= 1 and sge("wo"):
                emit_ogT_s(n_tiles - 1, last=True)


            do_cc = sge("cc")
            am_red = (em.op("vector", (lambda eng: eng.reduce_max(
                stats[:, 8:9], amaxb[:, 0:n_tiles], AX.X)), inc=("sv", 1))
                if do_cc else None)
            g1 = (em.op("gpsimd", (lambda eng: eng.dma_start(
                red_scr[:, :], stats[:, 8:9])),
                waits=(("sv", am_red),), inc=("dgp", 16)) if do_cc else None)
            g2 = (em.op("gpsimd", (lambda eng: eng.dma_start(
                rowbuf[:, :], bass.AP(red_scr, 0, [[1, 1], [1, 128]]))),
                waits=(("dgp", g1),), inc=("dgp", 16)) if do_cc else None)
            am_sc = (em.op("vector", (lambda eng: eng.reduce_max(
                stats[0:1, 8:9], rowbuf[:, :], AX.X)),
                waits=(("dgp", g2),), inc=("sv", 1)) if do_cc else None)
            cc_st = (em.op("gpsimd", (lambda eng: eng.dma_start(
                cc_in[:, :], stats[0:1, 8:9])),
                waits=(("sv", am_sc),), inc=("dgp", 16)) if do_cc else None)
            cc_done = (em.op("gpsimd", (lambda eng: eng.collective_compute(
                "AllReduce", ALU.max,
                replica_groups=[list(range(n_cores))],
                ins=[cc_in.ap().opt()], outs=[cc_out.ap().opt()])),
                waits=(("dgp", cc_st),), inc=("sq", 1)) if do_cc else None)

            p1_final = [(k, em.val(k)) for k in
                        ("dcs", "dx0", "dx1", "dt0", "dt1", "dsx", "dss",
                         "dgp", "sv", "ss", "st") if em.val(k) > 0]

            em.replay_block(nc, sems)

        if not sge("full"):
            emx = Em(em.cnt)
            emx.op("gpsimd", (lambda eng, xb0=xb[0]: eng.dma_start(
                xb0[:, :], x_d[0:128, :])),
                waits=p1_final, inc=("dgp", 16))
            emx.op("gpsimd", (lambda eng, xb0=xb[0]: eng.dma_start(
                out_d[0:128, :], xb0[:, :])),
                waits=(("dgp", emx.val("dgp")),), inc=("dgp", 16))
            emx.op("gpsimd", lambda eng: None,
                   waits=[(k, emx.val(k)) for k in
                          ("dgp", "sv", "ss", "st", "dcs", "dx0", "dx1",
                           "dt0", "dt1", "dsx", "dss") if emx.val(k) > 0])
            emx.replay_block(nc, sems)
            return nc

        em2 = Em(em.cnt)
        lm2 = {}
        with ExitStack() as p2:
            e2 = p2.enter_context
            wfst = e2(nc.sbuf_tensor("wfst", [128, 8 * D], F32))
            wq8 = e2(nc.sbuf_tensor("wq8", [128, 8 * D], FP8))
            bias_t = e2(nc.sbuf_tensor("bias_t", [128, D], F32))
            xnt2 = [e2(nc.sbuf_tensor(f"xnt2_{i}", [128, D], F32))
                    for i in range(2)]
            xb2 = [e2(nc.sbuf_tensor(f"xb2_{i}", [128, D], F32))
                   for i in range(2)]
            sb2 = [e2(nc.sbuf_tensor(f"sb2_{i}", [128, D], BF16))
                   for i in range(2)]
            xq8 = [e2(nc.sbuf_tensor(f"xq8_{i}", [128, D], FP8))
                   for i in range(2)]
            ysb = [e2(nc.sbuf_tensor(f"ysb_{i}", [128, D], F32))
                   for i in range(2)]
            gff = [e2(nc.sbuf_tensor(f"gff_{i}", [128, D], BF16))
                   for i in range(2)]
            prod = [e2(nc.sbuf_tensor(f"prod_{i}", [128, D], F32))
                    for i in range(2)]
            outb = [e2(nc.sbuf_tensor(f"outb_{i}", [128, D], F32))
                    for i in range(2)]
            ps_y = [e2(nc.psum_tensor(f"ps_y{i}", [128, D], F32))
                    for i in range(2)]

            for s in range(8):
                em2.op("gpsimd", (lambda eng, s_=s: eng.dma_start(
                    wfst[:, s_ * D:(s_ + 1) * D],
                    wft_d[s_ * 128:(s_ + 1) * 128, :])),
                    waits=(p1_final if s == 0 else ()), inc=("dgp", 16))
            wf_ld = em2.val("dgp")

            amw_red = em2.op("vector", (lambda eng: eng.reduce_max(
                stats[:, 13:14], wfst[:, :], AX.X,
                apply_absolute_value=True)),
                waits=[("dgp", wf_ld)] + p1_final, inc=("sv", 1))
            h1 = em2.op("gpsimd", (lambda eng: eng.dma_start(
                red_scr2[:, :], stats[:, 13:14])),
                waits=(("sv", amw_red),), inc=("dgp", 16))
            h2 = em2.op("gpsimd", (lambda eng: eng.dma_start(
                rowbuf[:, :], bass.AP(red_scr2, 0, [[1, 1], [1, 128]]))),
                waits=(("dgp", h1),), inc=("dgp", 16))
            amw_sc = em2.op("vector", (lambda eng: eng.reduce_max(
                stats[0:1, 13:14], rowbuf[:, :], AX.X)),
                waits=(("dgp", h2),), inc=("sv", 1))
            h3 = em2.op("gpsimd", (lambda eng: eng.dma_start(
                wmx_scr[:, :], stats[0:1, 13:14])),
                waits=(("sv", amw_sc),), inc=("dgp", 16))
            h4 = em2.op("gpsimd", (lambda eng: eng.dma_start(
                amaxb[:, 1:2], bass.AP(wmx_scr, 0, [[0, 128], [1, 1]]))),
                waits=(("dgp", h3),), inc=("dgp", 16))
            amg_bc = em2.op("gpsimd", (lambda eng: eng.dma_start(
                stats[:, 9:10], bass.AP(cc_out, 0, [[0, 128], [1, 1]]))),
                waits=(("sq", cc_done), ("dgp", h4)), inc=("dgp", 16))
            k1 = em2.op("vector", (lambda eng: eng.tensor_scalar(
                stats[:, 9:10], stats[:, 9:10], 1e-12, None, ALU.max)),
                waits=(("dgp", amg_bc),), inc=("sv", 1))
            k2 = em2.op("vector", (lambda eng: eng.reciprocal(
                stats[:, 10:11], stats[:, 9:10])), waits=(("sv", k1),),
                inc=("sv", 1))
            k3 = em2.op("vector", (lambda eng: eng.tensor_scalar(
                stats[:, 10:11], stats[:, 10:11], 224.0, None, ALU.mult)),
                waits=(("sv", k2),), inc=("sv", 1))
            k4 = em2.op("vector", (lambda eng: eng.tensor_scalar(
                amaxb[:, 1:2], amaxb[:, 1:2], 1e-12, None, ALU.max)),
                waits=(("sv", k3),), inc=("sv", 1))
            k5 = em2.op("vector", (lambda eng: eng.reciprocal(
                stats[:, 11:12], amaxb[:, 1:2])), waits=(("sv", k4),),
                inc=("sv", 1))
            k6 = em2.op("vector", (lambda eng: eng.tensor_scalar(
                stats[:, 11:12], stats[:, 11:12], 224.0, None, ALU.mult)),
                waits=(("sv", k5),), inc=("sv", 1))
            k7 = em2.op("vector", (lambda eng: eng.tensor_tensor(
                stats[:, 12:13], stats[:, 10:11], stats[:, 11:12], ALU.mult)),
                waits=(("sv", k6),), inc=("sv", 1))
            sc_done = em2.op("vector", (lambda eng: eng.reciprocal(
                stats[:, 12:13], stats[:, 12:13])), waits=(("sv", k7),),
                inc=("sv", 1))
            wq8_done = em2.op("scalar", (lambda eng: eng.activation(
                wq8[:, :], wfst[:, :], AF.Identity, scale=stats[:, 11:12])),
                waits=[("sv", sc_done)] + p1_final, inc=("ss", 1))
            bias_bc = em2.op("gpsimd", (lambda eng: eng.dma_start(
                bias_t[:, :], bass.AP(bias_d, 0, [[0, 128], [1, D]]))),
                waits=tuple(p1_final), inc=("dgp", 16))

            for t in range(n_tiles if sge("b1") else 0):
                i = t % 2
                r0 = t * 128
                asem, bsem, psem, osem = (f"da{i}", f"db{i}", f"dp{i}",
                                          f"do{i}")
                xnt_in = em2.op("sync", (lambda eng, i_=i, r=r0: eng.dma_start(
                    xnt2[i_][:, :], xnt_scr[r:r + 128, :])),
                    waits=([("ss", lm2[("xq", t - 2)])] if t >= 2
                           else [("sv", sc_done)]),
                    inc=(asem, 16))
                x_in = em2.op("sync", (lambda eng, i_=i, r=r0: eng.dma_start(
                    xb2[i_][:, :], x_d[r:r + 128, :])),
                    waits=([("sv", lm2[("comb", t - 2)])] if t >= 2 else []),
                    inc=(bsem, 16))
                s_in = em2.op("gpsimd", (lambda eng, i_=i, r=r0: eng.dma_start(
                    sb2[i_][:, :], s_scr[r:r + 128, :])),
                    waits=([("sv", lm2[("prod", t - 2)])] if t >= 2 else []),
                    inc=(psem, 16))
                lm2[("xq", t)] = em2.op("scalar",
                                        (lambda eng, i_=i: eng.activation(
                                            xq8[i_][:, :], xnt2[i_][:, :],
                                            AF.Identity,
                                            scale=stats[:, 10:11])),
                                        waits=((asem, xnt_in),
                                               ("ss", wq8_done)),
                                        inc=("ss", 1))
                w_mm = [("ss", lm2[("xq", t)])]
                if t >= 2:
                    w_mm.append(("sv", lm2[("yev", t - 2)]))
                for s in range(8):
                    em2.op("tensor", (lambda eng, i_=i, s_=s: eng.matmul(
                        ps_y[i_][:, 0:512],
                        xq8[i_][:, s_ * 128:(s_ + 1) * 128],
                        wq8[:, s_ * D:s_ * D + 512],
                        start=(s_ == 0), stop=(s_ == 7))),
                        waits=(w_mm if s == 0 else ()))
                    em2.op("tensor", (lambda eng, i_=i, s_=s: eng.matmul(
                        ps_y[i_][:, 512:1024],
                        xq8[i_][:, s_ * 128:(s_ + 1) * 128],
                        wq8[:, s_ * D + 512:(s_ + 1) * D],
                        start=(s_ == 0), stop=(s_ == 7))),
                        inc=("st", 1) if s == 7 else None)
                y_mm = em2.val("st")
                lm2[("yev", t)] = em2.op(
                    "vector", (lambda eng, i_=i: eng.scalar_tensor_tensor(
                        ysb[i_][:, :], ps_y[i_][:, :], stats[:, 12:13],
                        bias_t[:, :], ALU.mult, ALU.add)),
                    waits=(("st", y_mm), ("dgp", bias_bc)), inc=("sv", 1))
                gff_done = em2.op("scalar", (lambda eng, i_=i: eng.activation(
                    gff[i_][:, :], ysb[i_][:, :], AF.Silu)),
                    waits=(("sv", lm2[("yev", t)]),), inc=("ss", 1))
                lm2[("prod", t)] = em2.op(
                    "vector", (lambda eng, i_=i: eng.tensor_tensor(
                        prod[i_][:, :], gff[i_][:, :], sb2[i_][:, :],
                        ALU.mult)),
                    waits=(("ss", gff_done), (psem, s_in)), inc=("sv", 1))
                lm2[("comb", t)] = em2.op(
                    "vector", (lambda eng, i_=i: eng.tensor_tensor(
                        outb[i_][:, :], xb2[i_][:, :], prod[i_][:, :],
                        ALU.add)),
                    waits=((bsem, x_in), ("sv", lm2[("prod", t)]),
                           ((osem, lm2[("ost", t - 2)]) if t >= 2
                            else ("sv", 0))),
                    inc=("sv", 1))
                lm2[("ost", t)] = em2.op(
                    "scalar", (lambda eng, i_=i, r=r0: eng.dma_start(
                        out_d[r:r + 128, :], outb[i_][:, :])),
                    waits=(("sv", lm2[("comb", t)]),), inc=(osem, 16))

            em2.op("scalar", lambda eng: None,
                   waits=(("do0", em2.val("do0")), ("do1", em2.val("do1"))))
            sv_f = em2.op("vector", lambda eng: eng.engine_nop(),
                          inc=("sv", 1))
            ss_f = em2.op("scalar", lambda eng: eng.copy(
                stats[0:1, 0:1], stats[0:1, 0:1]), inc=("ss", 1))
            em2.op("gpsimd", lambda eng: None,
                   waits=[("sv", sv_f), ("ss", ss_f), ("st", em2.val("st"))]
                   + [(k, em2.val(k)) for k in
                      ("da0", "da1", "db0", "db1", "dp0", "dp1",
                       "do0", "do1", "dgp") if em2.val(k) > 0])
            em2.replay_block(nc, sems)

    return nc


def _prep_inputs(Wq, Wk, Wv, Wg, Wo, gnw, Wf, bias, n_tiles):
    def perm_qk(W):
        Wp = W.reshape(H, 2, 128, D).transpose(1, 0, 2, 3).reshape(D, D)
        return np.ascontiguousarray(Wp.T)
    return {
        "wqt": perm_qk(np.asarray(Wq)),
        "wkt": perm_qk(np.asarray(Wk)),
        "wvt": np.ascontiguousarray(np.asarray(Wv).T),
        "wgt": np.ascontiguousarray(np.asarray(Wg).T),
        "wot": np.ascontiguousarray(np.asarray(Wo).T),
        "wft": np.ascontiguousarray(np.asarray(Wf).T),
        "fbias": np.asarray(bias).reshape(1, D).astype(np.float32),
        "gnw": np.ascontiguousarray(
            np.asarray(gnw).reshape(4, 128).T.astype(np.float32)),
        "idf": np.eye(128, dtype=np.float32),
        "mskt": np.triu(np.ones((128, 128), np.float32)),
        "tabs": host_tables(n_tiles * C),
    }


def kernel(L0_attn_x, L0_attn_Wq, L0_attn_Wk, L0_attn_Wv, L0_attn_Wg,
           L0_attn_Wo, L0_attn_gnorm_w, L0_ffn_weight, L0_ffn_bias):
    n_tiles = 64
    n_cores = 8
    x = np.asarray(L0_attn_x)
    base = _prep_inputs(L0_attn_Wq, L0_attn_Wk, L0_attn_Wv, L0_attn_Wg,
                        L0_attn_Wo, L0_attn_gnorm_w, L0_ffn_weight,
                        L0_ffn_bias, n_tiles)
    nc = build(n_tiles, n_cores)
    in_maps = [dict(base, x=np.ascontiguousarray(x[b]))
               for b in range(n_cores)]
    res = run_bass_kernel_spmd(nc, in_maps, core_ids=list(range(n_cores)))
    out = np.stack([res.results[b]["out"] for b in range(n_cores)], axis=0)
    return out.astype(np.float32)


# version 08
# speedup vs baseline: 1.1122x; 1.1064x over previous
import numpy as np
import ml_dtypes
from contextlib import ExitStack

import concourse.bass as bass
import concourse.mybir as mybir
from concourse.bass_utils import run_bass_kernel_spmd

F32 = mybir.dt.float32
BF16 = mybir.dt.bfloat16
FP8 = mybir.dt.float8e4
AF = mybir.ActivationFunctionType
ALU = mybir.AluOpType
AX = mybir.AxisListType

D = 1024
H = 4
HV = 512
C = 128
EPS = 1e-5

ENGS = ("sync", "gpsimd", "vector", "scalar", "tensor")


class Em:

    def __init__(self, cnt=None):
        self.cmds = {e: [] for e in ENGS}
        self.cnt = dict(cnt) if cnt else {}

    def val(self, sem):
        return self.cnt.get(sem, 0)

    def op(self, eng, build_fn, waits=(), inc=None):
        waits = tuple(waits)

        def cmd(engine, sems):
            for sname, v in waits:
                if v > 0:
                    engine.wait_ge(sems[sname], v)
            instr = build_fn(engine)
            if inc is not None and instr is not None:
                instr.then_inc(sems[inc[0]], inc[1])
        self.cmds[eng].append(cmd)
        if inc is not None:
            self.cnt[inc[0]] = self.cnt.get(inc[0], 0) + inc[1]
            return self.cnt[inc[0]]
        return None

    def replay_block(self, nc, sems):
        with nc.Block() as block:
            for name in ENGS:
                def mk(n):
                    def run(engine):
                        for cmd in self.cmds[n]:
                            cmd(engine, sems)
                    return run
                getattr(block, name)(mk(name))


def _gammas():
    return [np.float32(1.0) - np.float32(2.0 ** (-5.0 - h)) for h in range(H)]


def host_tables(T):
    j = np.arange(0, 256, 2, dtype=np.float32) / np.float32(256)
    inv = 1.0 / (10000.0 ** j.astype(np.float64))
    freqs = (np.arange(T, dtype=np.float32)[:, None]
             * inv.astype(np.float32)[None, :]).astype(np.float32)
    cos64 = np.cos(freqs.astype(np.float64))
    sin64 = np.sin(freqs.astype(np.float64))
    i_loc = (np.arange(T) % C).astype(np.float64)
    blocks = []
    for kind in range(4):
        for h in range(H):
            g = np.float64(_gammas()[h])
            if kind < 2:
                scale = (g ** i_loc) / 16.0
                base = cos64 if kind == 0 else sin64
            else:
                scale = g ** (-i_loc)
                base = cos64 if kind == 2 else sin64
            blocks.append((base * scale[:, None]).astype(np.float32))
    return np.ascontiguousarray(np.concatenate(blocks, axis=1))


_STAGES = ("w", "b1", "b2", "b3", "b4", "ba", "bb", "bc", "base", "proj", "vg", "tq", "ret", "wo", "cc", "full")


def build(n_tiles, n_cores, stage="full", dbg=False):
    sidx = _STAGES.index(stage)

    def sge(name):
        return sidx >= _STAGES.index(name)

    T = n_tiles * C
    nc = bass.Bass("TRN2", target_bir_lowering=False,
                   detect_race_conditions=False)

    x_d = nc.dram_tensor("x", [T, D], F32, kind="ExternalInput")
    wqt_d = nc.dram_tensor("wqt", [D, D], F32, kind="ExternalInput")
    wkt_d = nc.dram_tensor("wkt", [D, D], F32, kind="ExternalInput")
    wvt_d = nc.dram_tensor("wvt", [D, 2 * D], F32, kind="ExternalInput")
    wgt_d = nc.dram_tensor("wgt", [D, 2 * D], F32, kind="ExternalInput")
    wot_d = nc.dram_tensor("wot", [2 * D, D], F32, kind="ExternalInput")
    wft_d = nc.dram_tensor("wft", [D, D], F32, kind="ExternalInput")
    bias_d = nc.dram_tensor("fbias", [1, D], F32, kind="ExternalInput")
    gnw_d = nc.dram_tensor("gnw", [128, 4], F32, kind="ExternalInput")
    idf_d = nc.dram_tensor("idf", [128, 128], F32, kind="ExternalInput")
    mskt_d = nc.dram_tensor("mskt", [128, 128], F32, kind="ExternalInput")
    tabs_d = nc.dram_tensor("tabs", [T, 2048], F32, kind="ExternalInput")

    out_d = nc.dram_tensor("out", [T, D], F32, kind="ExternalOutput")

    scr_kind = "ExternalOutput" if dbg else "Internal"
    xnt_scr = nc.dram_tensor("xnt_scr", [T, D], F32, kind=scr_kind)
    s_scr = nc.dram_tensor("s_scr", [T, D], BF16, kind=scr_kind)
    if dbg:
        xh_scr = nc.dram_tensor("xh_scr", [T, D], BF16, kind="ExternalOutput")
        xl_scr = nc.dram_tensor("xl_scr", [T, D], BF16, kind="ExternalOutput")
    cc_in = nc.dram_tensor("cc_in", [1, 1], F32)
    cc_out = nc.dram_tensor("cc_out", [1, 1], F32)
    red_scr = nc.dram_tensor("red_scr", [128, 1], F32)
    red_scr2 = nc.dram_tensor("red_scr2", [128, 1], F32)
    wmx_scr = nc.dram_tensor("wmx_scr", [1, 1], F32)

    gams = _gammas()
    gCs = [float(np.float32(np.float64(g) ** C)) for g in gams]

    with ExitStack() as top:
        e = top.enter_context
        idb = e(nc.sbuf_tensor("idb_sb", [128, 128], BF16))
        mskt = e(nc.sbuf_tensor("mskt_sb", [128, 128], BF16))
        gnw = e(nc.sbuf_tensor("gnw_sb", [128, 4], F32))
        amaxb = e(nc.sbuf_tensor("amaxb", [128, max(n_tiles, 2)], F32))
        stats = e(nc.sbuf_tensor("stats", [128, 16], F32))
        gvar = e(nc.sbuf_tensor("gvar", [128, 4], F32))
        gv4 = e(nc.sbuf_tensor("gv4", [128, 4], F32))
        rtg = e(nc.sbuf_tensor("rtg", [128, 4], F32))
        rsg = e(nc.sbuf_tensor("rsg", [128, 4], F32))
        rowbuf = e(nc.sbuf_tensor("rowbuf", [1, 128], F32))

        sems = {}
        for s in ("dxa", "dcs", "dx0", "dx1", "dt0", "dt1", "dsx", "dss",
                  "dsc", "dgp", "da0", "da1", "db0", "db1", "dp0", "dp1",
                  "do0", "do1", "sv", "ss", "st", "sq"):
            sems[s] = e(nc.semaphore(s))

        em = Em()
        lm = {}

        SEMR = range(155, 256)
        em.op("gpsimd", lambda eng: eng.sem_clear(SEMR))
        em.op("gpsimd", (lambda eng: eng.dma_start(
            rowbuf[0:1, 0:1], idf_d[0:1, 0:1])), inc=("dgp", 16))
        for _eng in ("sync", "vector", "scalar", "tensor"):
            em.op(_eng, lambda eng: None, waits=(("dgp", 16),))

        def lw(sem, key):
            return [(sem, lm[key])] if key in lm else []

        with ExitStack() as p1:
            e1 = p1.enter_context
            wq = e1(nc.sbuf_tensor("wq_sb", [128, 8 * D], BF16))
            wk = e1(nc.sbuf_tensor("wk_sb", [128, 8 * D], BF16))
            wv = e1(nc.sbuf_tensor("wv_sb", [128, 16 * D], BF16))
            wg = e1(nc.sbuf_tensor("wg_sb", [128, 16 * D], BF16))
            wo = e1(nc.sbuf_tensor("wo_sb", [128, 16 * D], BF16))
            Tst = e1(nc.sbuf_tensor("Tst", [128, 4096], F32))
            xb = [e1(nc.sbuf_tensor(f"xb{i}", [128, D], F32)) for i in range(2)]
            tb = [e1(nc.sbuf_tensor("tb0", [128, 2048], F32))]
            xn = e1(nc.sbuf_tensor("xn", [128, D], F32))
            xntb = e1(nc.sbuf_tensor("xntb", [128, D], BF16))
            xh = e1(nc.sbuf_tensor("xh", [128, D], BF16))
            qp = e1(nc.sbuf_tensor("qp", [128, D], BF16))
            kp = e1(nc.sbuf_tensor("kp", [128, D], BF16))
            qpT = e1(nc.sbuf_tensor("qpT", [128, D], BF16))
            kpT = e1(nc.sbuf_tensor("kpT", [128, D], BF16))
            vb = e1(nc.sbuf_tensor("vb", [128, 2 * D], BF16))
            mb = e1(nc.sbuf_tensor("mb", [128, 2 * D], BF16))
            Sbf = e1(nc.sbuf_tensor("Sbf", [128, 1024], BF16))
            atb = e1(nc.sbuf_tensor("atb", [128, 128], BF16))
            obf = e1(nc.sbuf_tensor("obf", [128, 2 * D], BF16))
            ogb = e1(nc.sbuf_tensor("ogb", [128, 2 * D], BF16))
            ogT = e1(nc.sbuf_tensor("ogT", [128, 2 * D], BF16))
            sbo = e1(nc.sbuf_tensor("sbo", [128, D], BF16))
            tmp1 = e1(nc.sbuf_tensor("tmp1", [128, 512], BF16))
            tmp2 = e1(nc.sbuf_tensor("tmp2", [128, 512], BF16))
            junk = e1(nc.sbuf_tensor("junk", [128, D], BF16))
            junk2 = e1(nc.sbuf_tensor("junk2", [128, HV], BF16))

            pA = e1(nc.psum_tensor("pA", [128, D], F32))
            pB = e1(nc.psum_tensor("pB", [128, D], F32))
            pC = e1(nc.psum_tensor("pC", [128, D], F32))
            pD = e1(nc.psum_tensor("pD", [128, D], F32))

            def wload(dst, src_d, nslab, ncol):
                for s in range(nslab):
                    em.op("gpsimd",
                          (lambda eng, d=dst, sd=src_d, s_=s, nco=ncol:
                           eng.dma_start(d[:, s_ * nco:(s_ + 1) * nco],
                                         sd[s_ * 128:(s_ + 1) * 128, :])),
                          inc=("dgp", 16))
            wload(wq, wqt_d, 8, D)
            wload(wk, wkt_d, 8, D)
            wload(wv, wvt_d, 8, 2 * D)
            wload(wg, wgt_d, 8, 2 * D)
            wload(wo, wot_d, 16, D)
            em.op("gpsimd", lambda eng: eng.dma_start(idb[:, :], idf_d[:, :]),
                  inc=("dgp", 16))
            em.op("gpsimd", lambda eng: eng.dma_start(mskt[:, :], mskt_d[:, :]),
                  inc=("dgp", 16))
            w_loaded = em.val("dgp")
            em.op("sync", lambda eng: eng.dma_start(gnw[:, :], gnw_d[:, :]),
                  inc=("dcs", 16))
            const_xa = em.val("dcs")

            gnw_done = None
            for s in range(16):
                gnw_done = em.op(
                    "vector",
                    (lambda eng, s_=s: eng.tensor_scalar(
                        wo[:, s_ * D:(s_ + 1) * D], wo[:, s_ * D:(s_ + 1) * D],
                        gnw[:, (s_ % 4):(s_ % 4) + 1], None, ALU.mult)),
                    waits=((("dgp", w_loaded), ("dcs", const_xa))
                           if s == 0 else ()),
                    inc=("sv", 1))
            tst_init = em.op("vector", lambda eng: eng.memset(Tst[:, :], 0.0),
                             inc=("sv", 1))

            def emit_ln(t):
                xcur = xb[t % 2]
                r0 = t * 128
                w_x = lw("ss", ("xfree", t - 2))
                xsem = f"dx{t % 2}"
                xin = em.op("sync",
                            (lambda eng, xc=xcur, r=r0:
                             eng.dma_start(xc[:, :], x_d[r:r + 128, :])),
                            waits=w_x, inc=(xsem, 16))

                sum_done = em.op("vector",
                                 (lambda eng, xc=xcur: eng.reduce_sum(
                                     stats[:, 0:1], xc[:, :], AX.X)),
                                 waits=((xsem, xin),), inc=("sv", 1))
                sq_done = em.op("scalar",
                                (lambda eng, xc=xcur: eng.activation(
                                    junk[:, :], xc[:, :], AF.Square,
                                    accum_out=stats[:, 1:2])),
                                waits=[(xsem, xin)]
                                + lw("st", ("xnt_mm", t - 1)),
                                inc=("ss", 1))
                c1 = em.op("vector", lambda eng: eng.tensor_scalar(
                    stats[:, 2:3], stats[:, 0:1], 1.0 / D, None, ALU.mult),
                    waits=(("sv", sum_done),), inc=("sv", 1))
                c2 = em.op("vector", lambda eng: eng.tensor_tensor(
                    stats[:, 13:14], stats[:, 2:3], stats[:, 2:3], ALU.mult),
                    waits=(("sv", c1),), inc=("sv", 1))
                c3 = em.op("vector", lambda eng: eng.scalar_tensor_tensor(
                    stats[:, 3:4], stats[:, 1:2], 1.0 / D, stats[:, 13:14],
                    ALU.mult, ALU.subtract),
                    waits=(("ss", sq_done), ("sv", c2)), inc=("sv", 1))
                var_done = em.op("vector", lambda eng: eng.tensor_scalar(
                    stats[:, 3:4], stats[:, 3:4], EPS, None, ALU.add),
                    waits=(("sv", c3),), inc=("sv", 1))
                rtv_done = em.op("scalar", lambda eng: eng.activation(
                    stats[:, 4:5], stats[:, 3:4], AF.Sqrt),
                    waits=(("sv", var_done),), inc=("ss", 1))
                c5 = em.op("vector", lambda eng: eng.reciprocal(
                    stats[:, 5:6], stats[:, 4:5]), waits=(("ss", rtv_done),),
                    inc=("sv", 1))
                nmr_done = em.op("vector", lambda eng: eng.scalar_tensor_tensor(
                    stats[:, 6:7], stats[:, 2:3], -1.0, stats[:, 5:6],
                    ALU.mult, ALU.mult), waits=(("sv", c5),), inc=("sv", 1))
                w_xn = ([("sv", nmr_done)] + lw("st", ("xnt_mm", t - 1))
                        + lw("dsx", ("xnts", t - 1))
                        + lw("sv", ("xntf_ev", t - 1)))
                xn_done = em.op("scalar", (lambda eng, xc=xcur: eng.activation(
                    xn[:, :], xc[:, :], AF.Identity,
                    bias=stats[:, 6:7], scale=stats[:, 5:6])),
                    waits=w_xn, inc=("ss", 1))
                lm[("xfree", t)] = xn_done
                em.op("vector", (lambda eng, t_=t: eng.reduce_max(
                    amaxb[:, t_:t_ + 1], xn[:, :], AX.X,
                    apply_absolute_value=True)),
                    waits=(("ss", xn_done),), inc=("sv", 1))

                hl_done = em.op("vector", lambda eng: eng.tensor_copy(
                    xh[:, :], xn[:, :]), waits=(("ss", xn_done),))
                hl_done = em.op("vector", lambda eng: eng.tensor_tensor(
                    junk[:, :], xn[:, :], xh[:, :], ALU.subtract),
                    inc=("sv", 1))
                lm[("hl", t)] = hl_done

            def emit_ogT_s(t, last=False):
                r0 = t * 128
                w_ogt = [("sv", lm[("og_done", t)]),
                         ("sv", lm[("tupd", t, H - 1)])]
                if last:
                    w_ogt += [("ss", lm[("o_ev", t, H - 1)]),
                              ("sv", lm[("at_ev", t, H - 1)])]
                else:
                    w_ogt += [("ss", lm[("xnt_ev", t + 1)]),
                              ("sv", lm[("xntf_ev", t + 1)])]
                for s_ in range(16):
                    dst = pA if s_ < 8 else pD
                    em.op("tensor", (lambda eng, s2=s_, d=dst: eng.matmul(
                        d[:, (s2 % 8) * 128:(s2 % 8) * 128 + 128],
                        ogb[:, s2 * 128:(s2 + 1) * 128], idb[:, :],
                        start=True, stop=True)),
                        waits=(w_ogt if s_ == 0 else ()),
                        inc=("st", 1) if s_ == 15 else None)
                lm[("ogt_mm", t)] = em.val("st")
                em.op("scalar", lambda eng: eng.copy(ogT[:, 0:1024], pA[:, :]),
                      waits=(("st", lm[("ogt_mm", t)]),))
                ogt_ev = em.op("scalar", lambda eng: eng.copy(
                    ogT[:, 1024:2048], pD[:, :]), inc=("ss", 1))
                lm[("ogt_ev", t)] = ogt_ev
                w_s = [("ss", ogt_ev)] + lw("dss", ("s_store", t - 1))
                for s_ in range(16):
                    em.op("tensor", (lambda eng, s2=s_: eng.matmul(
                        pD[:, 0:512], ogT[:, s2 * 128:(s2 + 1) * 128],
                        wo[:, s2 * D:s2 * D + 512],
                        start=(s2 == 0), stop=(s2 == 15))),
                        waits=(w_s if s_ == 0 else ()))
                    em.op("tensor", (lambda eng, s2=s_: eng.matmul(
                        pD[:, 512:1024], ogT[:, s2 * 128:(s2 + 1) * 128],
                        wo[:, s2 * D + 512:(s2 + 1) * D],
                        start=(s2 == 0), stop=(s2 == 15))),
                        inc=("st", 1) if s_ == 15 else None)
                lm[("s_mm", t)] = em.val("st")
                lm[("s_ev", t)] = em.op(
                    "vector", (lambda eng: eng.tensor_copy(
                        sbo[:, :], pD[:, :])),
                    waits=(("st", lm[("s_mm", t)]),), inc=("sv", 1))
                lm[("s_store", t)] = em.op(
                    "scalar", (lambda eng, r=r0: eng.dma_start(
                        s_scr[r:r + 128, :], sbo[:, :])),
                    waits=(("sv", lm[("s_ev", t)]),), inc=("dss", 16))


            if sge("b1"):
                emit_ln(0)

            for t in range(n_tiles if sge("b1") else 0):
                tcur = tb[0]
                r0 = t * 128

                w_t = lw("sv", ("tfree", t - 1))
                tsem = f"dt{t % 2}"
                tin = em.op("sync",
                            (lambda eng, tc=tcur, r=r0:
                             eng.dma_start(tc[:, :], tabs_d[r:r + 128, :])),
                            waits=w_t, inc=(tsem, 16))

                if not sge("bb"):
                    continue
                hl_done = lm[("hl", t)]
                if dbg:
                    em.op("gpsimd", (lambda eng, r=r0: eng.dma_start(
                        xh_scr[r:r + 128, :], xh[:, :])),
                        waits=(("sv", hl_done),), inc=("dgp", 16))
                    em.op("gpsimd", (lambda eng, r=r0: eng.dma_start(
                        xl_scr[r:r + 128, :], junk[:, :])),
                        waits=(("sv", hl_done),), inc=("dgp", 16))
                w_tr = ([("sv", hl_done), ("dgp", w_loaded)]
                        + lw("ss", ("o_ev", t - 1, H - 1))
                        + lw("sv", ("at_ev", t - 1, H - 1))
                        + lw("ss", ("xnt_ev", t - 1))
                        + lw("sv", ("xntf_ev", t - 1)))
                for s in range(8):
                    em.op("tensor", (lambda eng, s_=s: eng.matmul(
                        pA[:, s_ * 128:(s_ + 1) * 128],
                        xh[:, s_ * 128:(s_ + 1) * 128], idb[:, :],
                        start=True, stop=False)),
                        waits=(w_tr if s == 0 else ()))
                    em.op("tensor", (lambda eng, s_=s: eng.matmul(
                        pA[:, s_ * 128:(s_ + 1) * 128],
                        junk[:, s_ * 128:(s_ + 1) * 128], idb[:, :],
                        start=False, stop=True)),
                        inc=("st", 1) if s == 7 else None)
                lm[("xnt_mm", t)] = em.val("st")
                if not sge("bc"):
                    continue
                w_evf = ([("st", lm[("xnt_mm", t)])]
                         + lw("dsx", ("xnts", t - 1)))
                xntf_ev = em.op("vector", lambda eng: eng.tensor_copy(
                    xn[:, :], pA[:, :]), waits=w_evf, inc=("sv", 1))
                lm[("xntf_ev", t)] = xntf_ev
                w_ev = ([("st", lm[("xnt_mm", t)]), ("sv", xntf_ev)]
                        + lw("st", ("g_mm", t - 1)))
                xnt_ev = em.op("scalar", lambda eng: eng.copy(
                    xntb[:, :], pA[:, :]), waits=w_ev, inc=("ss", 1))
                lm[("xnt_ev", t)] = xnt_ev
                if not sge("base"):
                    continue
                lm[("xnts", t)] = em.op("scalar",
                                        (lambda eng, r=r0: eng.dma_start(
                                            xnt_scr[r:r + 128, :],
                                            xn[:, :])),
                                        waits=(("sv", xntf_ev),),
                                        inc=("dsx", 16))

                if not sge("proj"):
                    continue
                def proj_round(wsrc, col0, name_mm, waits_first,
                               dlo=None, dhi=None):
                    dlo = dlo if dlo is not None else pB
                    dhi = dhi if dhi is not None else pC
                    for s in range(8):
                        base = col0 + s * (wsrc.shape[1] // 8)
                        for blk in range(4):
                            dst = (dlo if blk < 2 else dhi)
                            dcol = (blk % 2) * 512
                            em.op("tensor",
                                  (lambda eng, s_=s, b_=base, bl=blk, d=dst,
                                   dc=dcol, w_=wsrc: eng.matmul(
                                       d[:, dc:dc + 512],
                                       xntb[:, s_ * 128:(s_ + 1) * 128],
                                       w_[:, b_ + bl * 512:b_ + bl * 512 + 512],
                                       start=(s_ == 0), stop=(s_ == 7))),
                                  waits=(waits_first if (s == 0 and blk == 0)
                                         else ()),
                                  inc=(("st", 1) if (s == 7 and blk == 3)
                                       else None))
                    return em.val("st")

                w_qk = [("ss", xnt_ev)]
                if t == 0:
                    w_qk += [("dgp", w_loaded), ("sv", gnw_done)]
                else:
                    w_qk += lw("sv", ("s_ev", t - 1))
                for s in range(8):
                    for blk in range(4):
                        dst = pB if blk < 2 else pC
                        wsrc = wq if blk < 2 else wk
                        dcol = (blk % 2) * 512
                        scol = (blk % 2) * 512
                        em.op("tensor",
                              (lambda eng, s_=s, bl=blk, d=dst, dc=dcol,
                               w_=wsrc, sc=scol: eng.matmul(
                                   d[:, dc:dc + 512],
                                   xntb[:, s_ * 128:(s_ + 1) * 128],
                                   w_[:, s_ * D + sc:s_ * D + sc + 512],
                                   start=(s_ == 0), stop=(s_ == 7))),
                              waits=(w_qk if (s == 0 and blk == 0) else ()),
                              inc=(("st", 1) if (s == 7 and blk == 3)
                                   else None))
                qk_mm = em.val("st")

                if t >= 1:
                    emit_ogT_s(t - 1)

                first_w = [("st", qk_mm), (tsem, tin), ("dcs", const_xa)]
                seq = [
                    (pB, qp, 0, 512, 0, 512, ALU.subtract, 0),
                    (pB, qp, 512, 0, 0, 512, ALU.add, 512),
                    (pC, kp, 0, 512, 1024, 1536, ALU.subtract, 0),
                    (pC, kp, 512, 0, 1024, 1536, ALU.add, 512),
                ]
                n_r = 0
                for (ps, dst, ac, bc, ct, st_, alu, oc) in seq:
                    em.op("vector", (lambda eng, p_=ps, a_=ac, c_=ct:
                                     eng.tensor_tensor(
                                         tmp1[:, :], p_[:, a_:a_ + 512],
                                         tcur[:, c_:c_ + 512], ALU.mult)),
                          waits=(first_w if n_r == 0 else ()))
                    n_r += 1
                    em.op("vector", (lambda eng, p_=ps, b_=bc, s2=st_:
                                     eng.tensor_tensor(
                                         tmp2[:, :], p_[:, b_:b_ + 512],
                                         tcur[:, s2:s2 + 512], ALU.mult)))
                    em.op("vector", (lambda eng, d=dst, o_=oc, a2=alu:
                                     eng.tensor_tensor(
                                         d[:, o_:o_ + 512], tmp1[:, :],
                                         tmp2[:, :], a2)),
                          inc=("sv", 1) if n_r == 4 else None)
                rope_done = em.val("sv")
                lm[("rope_done", t)] = rope_done
                lm[("tfree", t)] = rope_done

                if not sge("vg"):
                    continue
                w_v = ([("ss", xnt_ev), ("sv", xntf_ev)]
                       + lw("sv", ("s_ev", t - 1)))
                v_mm = proj_round(wv, 0, "v", w_v, dlo=pA, dhi=pD)
                w_vev = [("st", v_mm)] + lw("sv", ("og_done", t - 1))
                em.op("scalar", lambda eng: eng.copy(vb[:, 0:1024], pA[:, :]),
                      waits=w_vev)
                v_ev = em.op("scalar", lambda eng: eng.copy(
                    vb[:, 1024:2048], pD[:, :]), inc=("ss", 1))
                g_mm = proj_round(wg, 0, "g", [("ss", v_ev)], dlo=pA, dhi=pD)
                lm[("g_mm", t)] = g_mm
                w_m = [("st", g_mm)] + lw("sv", ("og_done", t - 1))
                em.op("scalar", lambda eng: eng.activation(
                    mb[:, 0:1024], pA[:, :], AF.Silu), waits=w_m)
                m_done = em.op("scalar", lambda eng: eng.activation(
                    mb[:, 1024:2048], pD[:, :], AF.Silu), inc=("ss", 1))
                lm[("m_done", t)] = m_done

                if not sge("tq"):
                    continue
                w_tq = [("sv", rope_done), ("dgp", w_loaded)]
                for s in range(8):
                    em.op("tensor", (lambda eng, s_=s: eng.matmul(
                        pB[:, s_ * 128:(s_ + 1) * 128],
                        qp[:, s_ * 128:(s_ + 1) * 128], idb[:, :],
                        start=True, stop=True)),
                        waits=(w_tq if s == 0 else ()))
                for s in range(8):
                    em.op("tensor", (lambda eng, s_=s: eng.matmul(
                        pC[:, s_ * 128:(s_ + 1) * 128],
                        kp[:, s_ * 128:(s_ + 1) * 128], idb[:, :],
                        start=True, stop=True)),
                        inc=("st", 1) if s == 7 else None)
                tqk_mm = em.val("st")
                em.op("scalar", lambda eng: eng.copy(qpT[:, :], pB[:, :]),
                      waits=(("st", tqk_mm),))
                qkT_ev = em.op("scalar", lambda eng: eng.copy(
                    kpT[:, :], pC[:, :]), inc=("ss", 1))
                lm[("qkT_ev", t)] = qkT_ev

                em.op("vector", lambda eng: eng.memset(gvar[:, :], 0.0),
                      waits=(("ss", qkT_ev),))

                if not sge("ret"):
                    continue
                for h in range(H):
                    gC = gCs[h]
                    w_P = [("ss", qkT_ev), ("ss", m_done)]
                    if h > 0:
                        w_P.append(("sv", lm[("tupd", t, h - 1)]))
                    em.op("tensor", (lambda eng, h_=h: eng.matmul(
                        pD[:, 0:512], kp[:, 128 * h_:128 * h_ + 128],
                        vb[:, 512 * h_:512 * h_ + 512],
                        start=True, stop=True)), waits=w_P)
                    em.op("tensor", (lambda eng, h_=h: eng.matmul(
                        pD[:, 512:1024],
                        kp[:, 512 + 128 * h_:512 + 128 * h_ + 128],
                        vb[:, 512 * h_:512 * h_ + 512],
                        start=True, stop=True)), inc=("st", 1))
                    P_mm = em.val("st")
                    w_at = []
                    if h > 0:
                        w_at.append(("sv", lm[("at_ev", t, h - 1)]))
                    else:
                        w_at += lw("ss", ("ogt_ev", t - 1))
                    em.op("tensor", (lambda eng, h_=h: eng.matmul(
                        pA[:, 0:128], kpT[:, 128 * h_:128 * h_ + 128],
                        qpT[:, 128 * h_:128 * h_ + 128],
                        start=True, stop=False)), waits=w_at)
                    em.op("tensor", (lambda eng, h_=h: eng.matmul(
                        pA[:, 0:128], kpT[:, 128 * (4 + h_):128 * (4 + h_) + 128],
                        qpT[:, 128 * (4 + h_):128 * (4 + h_) + 128],
                        start=False, stop=True)), inc=("st", 1))
                    at_mm = em.val("st")
                    w_sb = [("sv", tst_init)] if (t, h) == (0, 0) else \
                        [("st", lm[("o_mm", t, h - 1)] if h > 0
                          else lm[("o_mm", t - 1, H - 1)])]
                    em.op("scalar", (lambda eng, h_=h, g_=gC: eng.activation(
                        Sbf[:, 0:512], Tst[:, 1024 * h_:1024 * h_ + 512],
                        AF.Identity, scale=float(g_))), waits=w_sb)
                    sbf_done = em.op("scalar",
                                     (lambda eng, h_=h, g_=gC: eng.activation(
                                         Sbf[:, 512:1024],
                                         Tst[:, 1024 * h_ + 512:1024 * (h_ + 1)],
                                         AF.Identity, scale=float(g_))),
                                     inc=("ss", 1))
                    em.op("vector", (lambda eng, h_=h, g_=gC:
                                     eng.scalar_tensor_tensor(
                                         Tst[:, 1024 * h_:1024 * h_ + 512],
                                         Tst[:, 1024 * h_:1024 * h_ + 512],
                                         g_, pD[:, 0:512],
                                         ALU.mult, ALU.add)),
                          waits=(("st", P_mm), ("ss", sbf_done)))
                    lm[("tupd", t, h)] = em.op(
                        "vector", (lambda eng, h_=h, g_=gC:
                                   eng.scalar_tensor_tensor(
                                       Tst[:, 1024 * h_ + 512:1024 * (h_ + 1)],
                                       Tst[:, 1024 * h_ + 512:1024 * (h_ + 1)],
                                       g_, pD[:, 512:1024],
                                       ALU.mult, ALU.add)), inc=("sv", 1))
                    lm[("at_ev", t, h)] = em.op(
                        "vector", (lambda eng: eng.tensor_tensor(
                            atb[:, :], pA[:, 0:128], mskt[:, :], ALU.mult)),
                        waits=(("st", at_mm),), inc=("sv", 1))
                    w_o = [("ss", sbf_done), ("sv", lm[("at_ev", t, h)])]
                    if h > 0:
                        w_o.append(("ss", lm[("o_ev", t, h - 1)]))
                    em.op("tensor", (lambda eng, h_=h: eng.matmul(
                        pA[:, 512:1024], qpT[:, 128 * h_:128 * h_ + 128],
                        Sbf[:, 0:512], start=True, stop=False)), waits=w_o)
                    em.op("tensor", (lambda eng, h_=h: eng.matmul(
                        pA[:, 512:1024],
                        qpT[:, 128 * (4 + h_):128 * (4 + h_) + 128],
                        Sbf[:, 512:1024], start=False, stop=False)))
                    em.op("tensor", (lambda eng, h_=h: eng.matmul(
                        pA[:, 512:1024], atb[:, :],
                        vb[:, 512 * h_:512 * h_ + 512],
                        start=False, stop=True)), inc=("st", 1))
                    lm[("o_mm", t, h)] = em.val("st")
                    em.op("scalar", (lambda eng, h_=h: eng.activation(
                        junk2[:, :], pA[:, 512:1024], AF.Square,
                        accum_out=gvar[:, h_:h_ + 1])),
                        waits=(("st", lm[("o_mm", t, h)]),))
                    w_oev = (lw("sv", ("og_done", t - 1))
                             if h == 0 else [])
                    lm[("o_ev", t, h)] = em.op(
                        "scalar", (lambda eng, h_=h: eng.copy(
                            obf[:, 512 * h_:512 * h_ + 512],
                            pA[:, 512:1024])), waits=w_oev, inc=("ss", 1))
                lm[("qkT_free", t)] = lm[("at_ev", t, H - 1)]

                if not sge("wo"):
                    continue
                gv_done = em.op("vector", lambda eng: eng.tensor_scalar(
                    gv4[:, :], gvar[:, :], 1.0 / HV, EPS, ALU.mult, ALU.add),
                    waits=(("ss", lm[("o_ev", t, H - 1)]),), inc=("sv", 1))
                rtg_done = em.op("scalar", lambda eng: eng.activation(
                    rtg[:, :], gv4[:, :], AF.Sqrt),
                    waits=(("sv", gv_done),), inc=("ss", 1))
                rsg_done = em.op("vector", lambda eng: eng.reciprocal(
                    rsg[:, :], rtg[:, :]), waits=(("ss", rtg_done),),
                    inc=("sv", 1))
                if t + 1 < n_tiles:
                    emit_ln(t + 1)
                og_done = None
                for h in range(H):
                    og_done = em.op(
                        "vector", (lambda eng, h_=h: eng.scalar_tensor_tensor(
                            ogb[:, 512 * h_:512 * h_ + 512],
                            obf[:, 512 * h_:512 * h_ + 512],
                            rsg[:, h_:h_ + 1],
                            mb[:, 512 * h_:512 * h_ + 512],
                            ALU.mult, ALU.mult)),
                        waits=((("sv", rsg_done), ("ss", m_done))
                               if h == 0 else ()),
                        inc=("sv", 1) if h == H - 1 else None)
                lm[("og_done", t)] = og_done


            if n_tiles >= 1 and sge("wo"):
                emit_ogT_s(n_tiles - 1, last=True)


            do_cc = sge("cc")
            am_red = (em.op("vector", (lambda eng: eng.reduce_max(
                stats[:, 8:9], amaxb[:, 0:n_tiles], AX.X)), inc=("sv", 1))
                if do_cc else None)
            g1 = (em.op("gpsimd", (lambda eng: eng.dma_start(
                red_scr[:, :], stats[:, 8:9])),
                waits=(("sv", am_red),), inc=("dgp", 16)) if do_cc else None)
            g2 = (em.op("gpsimd", (lambda eng: eng.dma_start(
                rowbuf[:, :], bass.AP(red_scr, 0, [[1, 1], [1, 128]]))),
                waits=(("dgp", g1),), inc=("dgp", 16)) if do_cc else None)
            am_sc = (em.op("vector", (lambda eng: eng.reduce_max(
                stats[0:1, 8:9], rowbuf[:, :], AX.X)),
                waits=(("dgp", g2),), inc=("sv", 1)) if do_cc else None)
            cc_st = (em.op("gpsimd", (lambda eng: eng.dma_start(
                cc_in[:, :], stats[0:1, 8:9])),
                waits=(("sv", am_sc),), inc=("dgp", 16)) if do_cc else None)
            cc_done = (em.op("gpsimd", (lambda eng: eng.collective_compute(
                "AllReduce", ALU.max,
                replica_groups=[list(range(n_cores))],
                ins=[cc_in.ap().opt()], outs=[cc_out.ap().opt()])),
                waits=(("dgp", cc_st),), inc=("sq", 1)) if do_cc else None)

            p1_final = [(k, em.val(k)) for k in
                        ("dcs", "dx0", "dx1", "dt0", "dt1", "dsx", "dss",
                         "dgp", "sv", "ss", "st") if em.val(k) > 0]

            em.replay_block(nc, sems)

        if not sge("full"):
            emx = Em(em.cnt)
            emx.op("gpsimd", (lambda eng, xb0=xb[0]: eng.dma_start(
                xb0[:, :], x_d[0:128, :])),
                waits=p1_final, inc=("dgp", 16))
            emx.op("gpsimd", (lambda eng, xb0=xb[0]: eng.dma_start(
                out_d[0:128, :], xb0[:, :])),
                waits=(("dgp", emx.val("dgp")),), inc=("dgp", 16))
            emx.op("gpsimd", lambda eng: None,
                   waits=[(k, emx.val(k)) for k in
                          ("dgp", "sv", "ss", "st", "dcs", "dx0", "dx1",
                           "dt0", "dt1", "dsx", "dss") if emx.val(k) > 0])
            emx.replay_block(nc, sems)
            return nc

        em2 = Em(em.cnt)
        lm2 = {}
        with ExitStack() as p2:
            e2 = p2.enter_context
            wfst = e2(nc.sbuf_tensor("wfst", [128, 8 * D], F32))
            wq8 = e2(nc.sbuf_tensor("wq8", [128, 8 * D], FP8))
            bias_t = e2(nc.sbuf_tensor("bias_t", [128, D], F32))
            xnt2 = [e2(nc.sbuf_tensor(f"xnt2_{i}", [128, D], F32))
                    for i in range(2)]
            xb2 = [e2(nc.sbuf_tensor(f"xb2_{i}", [128, D], F32))
                   for i in range(2)]
            sb2 = [e2(nc.sbuf_tensor(f"sb2_{i}", [128, D], BF16))
                   for i in range(2)]
            xq8 = [e2(nc.sbuf_tensor(f"xq8_{i}", [128, D], FP8))
                   for i in range(2)]
            ysb = [e2(nc.sbuf_tensor(f"ysb_{i}", [128, D], F32))
                   for i in range(2)]
            gff = [e2(nc.sbuf_tensor(f"gff_{i}", [128, D], BF16))
                   for i in range(2)]
            prod = [e2(nc.sbuf_tensor(f"prod_{i}", [128, D], F32))
                    for i in range(2)]
            outb = [e2(nc.sbuf_tensor(f"outb_{i}", [128, D], F32))
                    for i in range(2)]
            ps_y = [e2(nc.psum_tensor(f"ps_y{i}", [128, D], F32))
                    for i in range(2)]

            for s in range(8):
                em2.op("gpsimd", (lambda eng, s_=s: eng.dma_start(
                    wfst[:, s_ * D:(s_ + 1) * D],
                    wft_d[s_ * 128:(s_ + 1) * 128, :])),
                    waits=(p1_final if s == 0 else ()), inc=("dgp", 16))
            wf_ld = em2.val("dgp")

            amw_red = em2.op("vector", (lambda eng: eng.reduce_max(
                stats[:, 13:14], wfst[:, :], AX.X,
                apply_absolute_value=True)),
                waits=[("dgp", wf_ld)] + p1_final, inc=("sv", 1))
            h1 = em2.op("gpsimd", (lambda eng: eng.dma_start(
                red_scr2[:, :], stats[:, 13:14])),
                waits=(("sv", amw_red),), inc=("dgp", 16))
            h2 = em2.op("gpsimd", (lambda eng: eng.dma_start(
                rowbuf[:, :], bass.AP(red_scr2, 0, [[1, 1], [1, 128]]))),
                waits=(("dgp", h1),), inc=("dgp", 16))
            amw_sc = em2.op("vector", (lambda eng: eng.reduce_max(
                stats[0:1, 13:14], rowbuf[:, :], AX.X)),
                waits=(("dgp", h2),), inc=("sv", 1))
            h3 = em2.op("gpsimd", (lambda eng: eng.dma_start(
                wmx_scr[:, :], stats[0:1, 13:14])),
                waits=(("sv", amw_sc),), inc=("dgp", 16))
            h4 = em2.op("gpsimd", (lambda eng: eng.dma_start(
                amaxb[:, 1:2], bass.AP(wmx_scr, 0, [[0, 128], [1, 1]]))),
                waits=(("dgp", h3),), inc=("dgp", 16))
            amg_bc = em2.op("gpsimd", (lambda eng: eng.dma_start(
                stats[:, 9:10], bass.AP(cc_out, 0, [[0, 128], [1, 1]]))),
                waits=(("sq", cc_done), ("dgp", h4)), inc=("dgp", 16))
            k1 = em2.op("vector", (lambda eng: eng.tensor_scalar(
                stats[:, 9:10], stats[:, 9:10], 1e-12, None, ALU.max)),
                waits=(("dgp", amg_bc),), inc=("sv", 1))
            k2 = em2.op("vector", (lambda eng: eng.reciprocal(
                stats[:, 10:11], stats[:, 9:10])), waits=(("sv", k1),),
                inc=("sv", 1))
            k3 = em2.op("vector", (lambda eng: eng.tensor_scalar(
                stats[:, 10:11], stats[:, 10:11], 224.0, None, ALU.mult)),
                waits=(("sv", k2),), inc=("sv", 1))
            k4 = em2.op("vector", (lambda eng: eng.tensor_scalar(
                amaxb[:, 1:2], amaxb[:, 1:2], 1e-12, None, ALU.max)),
                waits=(("sv", k3),), inc=("sv", 1))
            k5 = em2.op("vector", (lambda eng: eng.reciprocal(
                stats[:, 11:12], amaxb[:, 1:2])), waits=(("sv", k4),),
                inc=("sv", 1))
            k6 = em2.op("vector", (lambda eng: eng.tensor_scalar(
                stats[:, 11:12], stats[:, 11:12], 224.0, None, ALU.mult)),
                waits=(("sv", k5),), inc=("sv", 1))
            k7 = em2.op("vector", (lambda eng: eng.tensor_tensor(
                stats[:, 12:13], stats[:, 10:11], stats[:, 11:12], ALU.mult)),
                waits=(("sv", k6),), inc=("sv", 1))
            sc_done = em2.op("vector", (lambda eng: eng.reciprocal(
                stats[:, 12:13], stats[:, 12:13])), waits=(("sv", k7),),
                inc=("sv", 1))
            wq8_done = em2.op("scalar", (lambda eng: eng.activation(
                wq8[:, :], wfst[:, :], AF.Identity, scale=stats[:, 11:12])),
                waits=[("sv", sc_done)] + p1_final, inc=("ss", 1))
            bias_bc = em2.op("gpsimd", (lambda eng: eng.dma_start(
                bias_t[:, :], bass.AP(bias_d, 0, [[0, 128], [1, D]]))),
                waits=tuple(p1_final), inc=("dgp", 16))

            def emit_p2_in(t):
                i = t % 2
                r0 = t * 128
                asem, bsem, psem = f"da{t % 2}", f"db{t % 2}", f"dp{t % 2}"
                xnt_in = em2.op("sync", (lambda eng, i_=i, r=r0: eng.dma_start(
                    xnt2[i_][:, :], xnt_scr[r:r + 128, :])),
                    waits=([("ss", lm2[("xq", t - 2)])] if t >= 2
                           else [("sv", sc_done)]),
                    inc=(asem, 16))
                x_in = em2.op("sync", (lambda eng, i_=i, r=r0: eng.dma_start(
                    xb2[i_][:, :], x_d[r:r + 128, :])),
                    waits=([("sv", lm2[("comb", t - 2)])] if t >= 2 else []),
                    inc=(bsem, 16))
                s_in = em2.op("gpsimd", (lambda eng, i_=i, r=r0: eng.dma_start(
                    sb2[i_][:, :], s_scr[r:r + 128, :])),
                    waits=([("sv", lm2[("prod", t - 2)])] if t >= 2 else []),
                    inc=(psem, 16))
                lm2[("xq", t)] = em2.op("scalar",
                                        (lambda eng, i_=i: eng.activation(
                                            xq8[i_][:, :], xnt2[i_][:, :],
                                            AF.Identity,
                                            scale=stats[:, 10:11])),
                                        waits=((asem, xnt_in),
                                               ("ss", wq8_done)),
                                        inc=("ss", 1))
                lm2[("xin", t)] = x_in
                lm2[("sin", t)] = s_in

            if sge("b1") and n_tiles > 0:
                emit_p2_in(0)
            for t in range(n_tiles if sge("b1") else 0):
                i = t % 2
                r0 = t * 128
                asem, bsem, psem, osem = (f"da{i}", f"db{i}", f"dp{i}",
                                          f"do{i}")
                x_in = lm2[("xin", t)]
                s_in = lm2[("sin", t)]
                w_mm = [("ss", lm2[("xq", t)])]
                if t >= 2:
                    w_mm.append(("sv", lm2[("yev", t - 2)]))
                for s in range(8):
                    em2.op("tensor", (lambda eng, i_=i, s_=s: eng.matmul(
                        ps_y[i_][:, 0:512],
                        xq8[i_][:, s_ * 128:(s_ + 1) * 128],
                        wq8[:, s_ * D:s_ * D + 512],
                        start=(s_ == 0), stop=(s_ == 7))),
                        waits=(w_mm if s == 0 else ()))
                    em2.op("tensor", (lambda eng, i_=i, s_=s: eng.matmul(
                        ps_y[i_][:, 512:1024],
                        xq8[i_][:, s_ * 128:(s_ + 1) * 128],
                        wq8[:, s_ * D + 512:(s_ + 1) * D],
                        start=(s_ == 0), stop=(s_ == 7))),
                        inc=("st", 1) if s == 7 else None)
                y_mm = em2.val("st")
                if t + 1 < n_tiles:
                    emit_p2_in(t + 1)
                lm2[("yev", t)] = em2.op(
                    "vector", (lambda eng, i_=i: eng.scalar_tensor_tensor(
                        ysb[i_][:, :], ps_y[i_][:, :], stats[:, 12:13],
                        bias_t[:, :], ALU.mult, ALU.add)),
                    waits=(("st", y_mm), ("dgp", bias_bc)), inc=("sv", 1))
                gff_done = em2.op("scalar", (lambda eng, i_=i: eng.activation(
                    gff[i_][:, :], ysb[i_][:, :], AF.Silu)),
                    waits=(("sv", lm2[("yev", t)]),), inc=("ss", 1))
                lm2[("prod", t)] = em2.op(
                    "vector", (lambda eng, i_=i: eng.tensor_tensor(
                        prod[i_][:, :], gff[i_][:, :], sb2[i_][:, :],
                        ALU.mult)),
                    waits=(("ss", gff_done), (psem, s_in)), inc=("sv", 1))
                lm2[("comb", t)] = em2.op(
                    "vector", (lambda eng, i_=i: eng.tensor_tensor(
                        outb[i_][:, :], xb2[i_][:, :], prod[i_][:, :],
                        ALU.add)),
                    waits=((bsem, x_in), ("sv", lm2[("prod", t)]),
                           ((osem, lm2[("ost", t - 2)]) if t >= 2
                            else ("sv", 0))),
                    inc=("sv", 1))
                lm2[("ost", t)] = em2.op(
                    "scalar", (lambda eng, i_=i, r=r0: eng.dma_start(
                        out_d[r:r + 128, :], outb[i_][:, :])),
                    waits=(("sv", lm2[("comb", t)]),), inc=(osem, 16))

            em2.op("scalar", lambda eng: None,
                   waits=(("do0", em2.val("do0")), ("do1", em2.val("do1"))))
            sv_f = em2.op("vector", lambda eng: eng.engine_nop(),
                          inc=("sv", 1))
            ss_f = em2.op("scalar", lambda eng: eng.copy(
                stats[0:1, 0:1], stats[0:1, 0:1]), inc=("ss", 1))
            em2.op("gpsimd", lambda eng: None,
                   waits=[("sv", sv_f), ("ss", ss_f), ("st", em2.val("st"))]
                   + [(k, em2.val(k)) for k in
                      ("da0", "da1", "db0", "db1", "dp0", "dp1",
                       "do0", "do1", "dgp") if em2.val(k) > 0])
            em2.replay_block(nc, sems)

    return nc


def _prep_inputs(Wq, Wk, Wv, Wg, Wo, gnw, Wf, bias, n_tiles):
    def perm_qk(W):
        Wp = W.reshape(H, 2, 128, D).transpose(1, 0, 2, 3).reshape(D, D)
        return np.ascontiguousarray(Wp.T)
    return {
        "wqt": perm_qk(np.asarray(Wq)),
        "wkt": perm_qk(np.asarray(Wk)),
        "wvt": np.ascontiguousarray(np.asarray(Wv).T),
        "wgt": np.ascontiguousarray(np.asarray(Wg).T),
        "wot": np.ascontiguousarray(np.asarray(Wo).T),
        "wft": np.ascontiguousarray(np.asarray(Wf).T),
        "fbias": np.asarray(bias).reshape(1, D).astype(np.float32),
        "gnw": np.ascontiguousarray(
            np.asarray(gnw).reshape(4, 128).T.astype(np.float32)),
        "idf": np.eye(128, dtype=np.float32),
        "mskt": np.triu(np.ones((128, 128), np.float32)),
        "tabs": host_tables(n_tiles * C),
    }


def kernel(L0_attn_x, L0_attn_Wq, L0_attn_Wk, L0_attn_Wv, L0_attn_Wg,
           L0_attn_Wo, L0_attn_gnorm_w, L0_ffn_weight, L0_ffn_bias):
    n_tiles = 64
    n_cores = 8
    x = np.asarray(L0_attn_x)
    base = _prep_inputs(L0_attn_Wq, L0_attn_Wk, L0_attn_Wv, L0_attn_Wg,
                        L0_attn_Wo, L0_attn_gnorm_w, L0_ffn_weight,
                        L0_ffn_bias, n_tiles)
    nc = build(n_tiles, n_cores)
    in_maps = [dict(base, x=np.ascontiguousarray(x[b]))
               for b in range(n_cores)]
    res = run_bass_kernel_spmd(nc, in_maps, core_ids=list(range(n_cores)))
    out = np.stack([res.results[b]["out"] for b in range(n_cores)], axis=0)
    return out.astype(np.float32)


# version 09 (final)
# speedup vs baseline: 1.1545x; 1.0381x over previous
import numpy as np
import ml_dtypes
from contextlib import ExitStack

import concourse.bass as bass
import concourse.mybir as mybir
from concourse.bass_utils import run_bass_kernel_spmd

F32 = mybir.dt.float32
BF16 = mybir.dt.bfloat16
FP8 = mybir.dt.float8e4
AF = mybir.ActivationFunctionType
ALU = mybir.AluOpType
AX = mybir.AxisListType

D = 1024
H = 4
HV = 512
C = 128
EPS = 1e-5

ENGS = ("sync", "gpsimd", "vector", "scalar", "tensor")


class Em:

    def __init__(self, cnt=None):
        self.cmds = {e: [] for e in ENGS}
        self.cnt = dict(cnt) if cnt else {}

    def val(self, sem):
        return self.cnt.get(sem, 0)

    def op(self, eng, build_fn, waits=(), inc=None):
        waits = tuple(waits)

        def cmd(engine, sems):
            for sname, v in waits:
                if v > 0:
                    engine.wait_ge(sems[sname], v)
            instr = build_fn(engine)
            if inc is not None and instr is not None:
                instr.then_inc(sems[inc[0]], inc[1])
        self.cmds[eng].append(cmd)
        if inc is not None:
            self.cnt[inc[0]] = self.cnt.get(inc[0], 0) + inc[1]
            return self.cnt[inc[0]]
        return None

    def replay_block(self, nc, sems):
        with nc.Block() as block:
            for name in ENGS:
                def mk(n):
                    def run(engine):
                        for cmd in self.cmds[n]:
                            cmd(engine, sems)
                    return run
                getattr(block, name)(mk(name))


def _gammas():
    return [np.float32(1.0) - np.float32(2.0 ** (-5.0 - h)) for h in range(H)]


def host_tables(T):
    j = np.arange(0, 256, 2, dtype=np.float32) / np.float32(256)
    inv = 1.0 / (10000.0 ** j.astype(np.float64))
    freqs = (np.arange(T, dtype=np.float32)[:, None]
             * inv.astype(np.float32)[None, :]).astype(np.float32)
    cos64 = np.cos(freqs.astype(np.float64))
    sin64 = np.sin(freqs.astype(np.float64))
    i_loc = (np.arange(T) % C).astype(np.float64)
    blocks = []
    for kind in range(4):
        for h in range(H):
            g = np.float64(_gammas()[h])
            if kind < 2:
                scale = (g ** i_loc) / 16.0
                base = cos64 if kind == 0 else sin64
            else:
                scale = g ** (-i_loc)
                base = cos64 if kind == 2 else sin64
            blocks.append((base * scale[:, None]).astype(np.float32))
    return np.ascontiguousarray(np.concatenate(blocks, axis=1))


_STAGES = ("w", "b1", "b2", "b3", "b4", "ba", "bb", "bc", "base", "proj", "vg", "tq", "ret", "wo", "cc", "full")


def build(n_tiles, n_cores, stage="full", dbg=False):
    sidx = _STAGES.index(stage)

    def sge(name):
        return sidx >= _STAGES.index(name)

    T = n_tiles * C
    nc = bass.Bass("TRN2", target_bir_lowering=False,
                   detect_race_conditions=False)

    x_d = nc.dram_tensor("x", [T, D], F32, kind="ExternalInput")
    wqt_d = nc.dram_tensor("wqt", [D, D], F32, kind="ExternalInput")
    wkt_d = nc.dram_tensor("wkt", [D, D], F32, kind="ExternalInput")
    wvt_d = nc.dram_tensor("wvt", [D, 2 * D], F32, kind="ExternalInput")
    wgt_d = nc.dram_tensor("wgt", [D, 2 * D], F32, kind="ExternalInput")
    wot_d = nc.dram_tensor("wot", [2 * D, D], F32, kind="ExternalInput")
    wft_d = nc.dram_tensor("wft", [D, D], F32, kind="ExternalInput")
    bias_d = nc.dram_tensor("fbias", [1, D], F32, kind="ExternalInput")
    gnw_d = nc.dram_tensor("gnw", [128, 4], F32, kind="ExternalInput")
    idf_d = nc.dram_tensor("idf", [128, 128], F32, kind="ExternalInput")
    mskt_d = nc.dram_tensor("mskt", [128, 128], F32, kind="ExternalInput")
    tabs_d = nc.dram_tensor("tabs", [T, 2048], F32, kind="ExternalInput")

    out_d = nc.dram_tensor("out", [T, D], F32, kind="ExternalOutput")

    scr_kind = "ExternalOutput" if dbg else "Internal"
    xnt_scr = nc.dram_tensor("xnt_scr", [T, D], F32, kind=scr_kind)
    s_scr = nc.dram_tensor("s_scr", [T, D], BF16, kind=scr_kind)
    if dbg:
        xh_scr = nc.dram_tensor("xh_scr", [T, D], BF16, kind="ExternalOutput")
        xl_scr = nc.dram_tensor("xl_scr", [T, D], BF16, kind="ExternalOutput")
    cc_in = nc.dram_tensor("cc_in", [1, 1], F32)
    cc_out = nc.dram_tensor("cc_out", [1, 1], F32)
    red_scr = nc.dram_tensor("red_scr", [128, 1], F32)
    red_scr2 = nc.dram_tensor("red_scr2", [128, 1], F32)
    wmx_scr = nc.dram_tensor("wmx_scr", [1, 1], F32)

    gams = _gammas()
    gCs = [float(np.float32(np.float64(g) ** C)) for g in gams]

    with ExitStack() as top:
        e = top.enter_context
        idb = e(nc.sbuf_tensor("idb_sb", [128, 128], BF16))
        mskt = e(nc.sbuf_tensor("mskt_sb", [128, 128], BF16))
        gnw = e(nc.sbuf_tensor("gnw_sb", [128, 4], F32))
        amaxb = e(nc.sbuf_tensor("amaxb", [128, max(n_tiles, 2)], F32))
        stats = e(nc.sbuf_tensor("stats", [128, 16], F32))
        gvar = e(nc.sbuf_tensor("gvar", [128, 4], F32))
        gv4 = e(nc.sbuf_tensor("gv4", [128, 4], F32))
        rtg = e(nc.sbuf_tensor("rtg", [128, 4], F32))
        rsg = e(nc.sbuf_tensor("rsg", [128, 4], F32))
        rowbuf = e(nc.sbuf_tensor("rowbuf", [1, 128], F32))

        sems = {}
        for s in ("dxa", "dcs", "dx0", "dx1", "dt0", "dt1", "dsx", "dss",
                  "dsc", "dgp", "da0", "da1", "db0", "db1", "dp0", "dp1",
                  "do0", "do1", "sv", "ss", "st", "sq"):
            sems[s] = e(nc.semaphore(s))

        em = Em()
        lm = {}

        SEMR = range(155, 256)
        em.op("gpsimd", lambda eng: eng.sem_clear(SEMR))
        em.op("gpsimd", (lambda eng: eng.dma_start(
            rowbuf[0:1, 0:1], idf_d[0:1, 0:1])), inc=("dgp", 16))
        for _eng in ("sync", "vector", "scalar", "tensor"):
            em.op(_eng, lambda eng: None, waits=(("dgp", 16),))

        def lw(sem, key):
            return [(sem, lm[key])] if key in lm else []

        with ExitStack() as p1:
            e1 = p1.enter_context
            wq = e1(nc.sbuf_tensor("wq_sb", [128, 8 * D], BF16))
            wk = e1(nc.sbuf_tensor("wk_sb", [128, 8 * D], BF16))
            wv = e1(nc.sbuf_tensor("wv_sb", [128, 16 * D], BF16))
            wg = e1(nc.sbuf_tensor("wg_sb", [128, 16 * D], BF16))
            wo = e1(nc.sbuf_tensor("wo_sb", [128, 16 * D], BF16))
            Tst = e1(nc.sbuf_tensor("Tst", [128, 4096], F32))
            xb = [e1(nc.sbuf_tensor(f"xb{i}", [128, D], F32)) for i in range(2)]
            tb = [e1(nc.sbuf_tensor("tb0", [128, 2048], F32))]
            xn = e1(nc.sbuf_tensor("xn", [128, D], F32))
            xntb = e1(nc.sbuf_tensor("xntb", [128, D], BF16))
            xh = e1(nc.sbuf_tensor("xh", [128, D], BF16))
            qp = e1(nc.sbuf_tensor("qp", [128, D], BF16))
            kp = e1(nc.sbuf_tensor("kp", [128, D], BF16))
            qpT = e1(nc.sbuf_tensor("qpT", [128, D], BF16))
            kpT = e1(nc.sbuf_tensor("kpT", [128, D], BF16))
            vb = e1(nc.sbuf_tensor("vb", [128, 2 * D], BF16))
            mb = e1(nc.sbuf_tensor("mb", [128, 2 * D], BF16))
            Sbf = e1(nc.sbuf_tensor("Sbf", [128, 1024], BF16))
            atb = e1(nc.sbuf_tensor("atb", [128, 128], BF16))
            obf = e1(nc.sbuf_tensor("obf", [128, 2 * D], BF16))
            ogb = e1(nc.sbuf_tensor("ogb", [128, 2 * D], BF16))
            ogT = e1(nc.sbuf_tensor("ogT", [128, 2 * D], BF16))
            sbo = e1(nc.sbuf_tensor("sbo", [128, D], BF16))
            tmp1 = e1(nc.sbuf_tensor("tmp1", [128, 512], BF16))
            tmp2 = e1(nc.sbuf_tensor("tmp2", [128, 512], BF16))
            junk = e1(nc.sbuf_tensor("junk", [128, D], BF16))
            junk2 = e1(nc.sbuf_tensor("junk2", [128, HV], BF16))

            pA = e1(nc.psum_tensor("pA", [128, D], F32))
            pB = e1(nc.psum_tensor("pB", [128, D], F32))
            pC = e1(nc.psum_tensor("pC", [128, D], F32))
            pD = e1(nc.psum_tensor("pD", [128, D], F32))

            def wload(dst, src_d, nslab, ncol):
                for s in range(nslab):
                    em.op("gpsimd",
                          (lambda eng, d=dst, sd=src_d, s_=s, nco=ncol:
                           eng.dma_start(d[:, s_ * nco:(s_ + 1) * nco],
                                         sd[s_ * 128:(s_ + 1) * 128, :])),
                          inc=("dgp", 16))
            wload(wq, wqt_d, 8, D)
            wload(wk, wkt_d, 8, D)
            wload(wv, wvt_d, 8, 2 * D)
            wload(wg, wgt_d, 8, 2 * D)
            wload(wo, wot_d, 16, D)
            em.op("gpsimd", lambda eng: eng.dma_start(idb[:, :], idf_d[:, :]),
                  inc=("dgp", 16))
            em.op("gpsimd", lambda eng: eng.dma_start(mskt[:, :], mskt_d[:, :]),
                  inc=("dgp", 16))
            w_loaded = em.val("dgp")
            em.op("sync", lambda eng: eng.dma_start(gnw[:, :], gnw_d[:, :]),
                  inc=("dcs", 16))
            const_xa = em.val("dcs")

            gnw_done = None
            for s in range(16):
                gnw_done = em.op(
                    "vector",
                    (lambda eng, s_=s: eng.tensor_scalar(
                        wo[:, s_ * D:(s_ + 1) * D], wo[:, s_ * D:(s_ + 1) * D],
                        gnw[:, (s_ % 4):(s_ % 4) + 1], None, ALU.mult)),
                    waits=((("dgp", w_loaded), ("dcs", const_xa))
                           if s == 0 else ()),
                    inc=("sv", 1))
            tst_init = em.op("vector", lambda eng: eng.memset(Tst[:, :], 0.0),
                             inc=("sv", 1))

            def emit_ln(t):
                xcur = xb[t % 2]
                r0 = t * 128
                w_x = lw("ss", ("xfree", t - 2))
                xsem = f"dx{t % 2}"
                xin = em.op("sync",
                            (lambda eng, xc=xcur, r=r0:
                             eng.dma_start(xc[:, :], x_d[r:r + 128, :])),
                            waits=w_x, inc=(xsem, 16))

                sum_done = em.op("vector",
                                 (lambda eng, xc=xcur: eng.reduce_sum(
                                     stats[:, 0:1], xc[:, :], AX.X)),
                                 waits=((xsem, xin),), inc=("sv", 1))
                sq_done = em.op("scalar",
                                (lambda eng, xc=xcur: eng.activation(
                                    junk[:, :], xc[:, :], AF.Square,
                                    accum_out=stats[:, 1:2])),
                                waits=[(xsem, xin)]
                                + lw("st", ("xnt_mm", t - 1)),
                                inc=("ss", 1))
                c1 = em.op("vector", lambda eng: eng.tensor_scalar(
                    stats[:, 2:3], stats[:, 0:1], 1.0 / D, None, ALU.mult),
                    waits=(("sv", sum_done),), inc=("sv", 1))
                c2 = em.op("vector", lambda eng: eng.tensor_tensor(
                    stats[:, 13:14], stats[:, 2:3], stats[:, 2:3], ALU.mult),
                    waits=(("sv", c1),), inc=("sv", 1))
                c3 = em.op("vector", lambda eng: eng.scalar_tensor_tensor(
                    stats[:, 3:4], stats[:, 1:2], 1.0 / D, stats[:, 13:14],
                    ALU.mult, ALU.subtract),
                    waits=(("ss", sq_done), ("sv", c2)), inc=("sv", 1))
                var_done = em.op("vector", lambda eng: eng.tensor_scalar(
                    stats[:, 3:4], stats[:, 3:4], EPS, None, ALU.add),
                    waits=(("sv", c3),), inc=("sv", 1))
                prev = em.op("vector", lambda eng: eng.tensor_scalar(
                    stats[:, 7:8], stats[:, 3:4], -0.5, None, ALU.mult),
                    waits=(("sv", var_done),), inc=("sv", 1))
                prev = em.op("vector", lambda eng: eng.tensor_scalar(
                    stats[:, 5:6], stats[:, 3:4], -0.5, 1.5, ALU.mult,
                    ALU.add), waits=(("sv", prev),), inc=("sv", 1))
                for _it in range(4):
                    prev = em.op("vector", lambda eng: eng.tensor_tensor(
                        stats[:, 13:14], stats[:, 5:6], stats[:, 5:6],
                        ALU.mult), waits=(("sv", prev),), inc=("sv", 1))
                    prev = em.op("vector", lambda eng: eng.tensor_scalar(
                        stats[:, 13:14], stats[:, 13:14], stats[:, 7:8],
                        1.5, ALU.mult, ALU.add),
                        waits=(("sv", prev),), inc=("sv", 1))
                    prev = em.op("vector", lambda eng: eng.tensor_tensor(
                        stats[:, 5:6], stats[:, 5:6], stats[:, 13:14],
                        ALU.mult), waits=(("sv", prev),), inc=("sv", 1))
                nmr_done = em.op("vector", lambda eng: eng.scalar_tensor_tensor(
                    stats[:, 6:7], stats[:, 2:3], -1.0, stats[:, 5:6],
                    ALU.mult, ALU.mult), waits=(("sv", prev),), inc=("sv", 1))
                w_xn = ([("sv", nmr_done)] + lw("st", ("xnt_mm", t - 1))
                        + lw("dsx", ("xnts", t - 1))
                        + lw("sv", ("xntf_ev", t - 1)))
                xn_done = em.op("scalar", (lambda eng, xc=xcur: eng.activation(
                    xn[:, :], xc[:, :], AF.Identity,
                    bias=stats[:, 6:7], scale=stats[:, 5:6])),
                    waits=w_xn, inc=("ss", 1))
                lm[("xfree", t)] = xn_done
                em.op("vector", (lambda eng, t_=t: eng.reduce_max(
                    amaxb[:, t_:t_ + 1], xn[:, :], AX.X,
                    apply_absolute_value=True)),
                    waits=(("ss", xn_done),), inc=("sv", 1))

                hl_done = em.op("vector", lambda eng: eng.tensor_copy(
                    xh[:, :], xn[:, :]), waits=(("ss", xn_done),))
                hl_done = em.op("vector", lambda eng: eng.tensor_tensor(
                    junk[:, :], xn[:, :], xh[:, :], ALU.subtract),
                    inc=("sv", 1))
                lm[("hl", t)] = hl_done

            def emit_ogT_s(t, last=False):
                r0 = t * 128
                w_ogt = [("sv", lm[("og_done", t)]),
                         ("sv", lm[("tupd", t, H - 1)])]
                if last:
                    w_ogt += [("ss", lm[("o_ev", t, H - 1)]),
                              ("sv", lm[("at_ev", t, H - 1)])]
                else:
                    w_ogt += [("ss", lm[("xnt_ev", t + 1)]),
                              ("sv", lm[("xntf_ev", t + 1)])]
                for s_ in range(16):
                    dst = pA if s_ < 8 else pD
                    em.op("tensor", (lambda eng, s2=s_, d=dst: eng.matmul(
                        d[:, (s2 % 8) * 128:(s2 % 8) * 128 + 128],
                        ogb[:, s2 * 128:(s2 + 1) * 128], idb[:, :],
                        start=True, stop=True)),
                        waits=(w_ogt if s_ == 0 else ()),
                        inc=("st", 1) if s_ == 15 else None)
                lm[("ogt_mm", t)] = em.val("st")
                em.op("scalar", lambda eng: eng.copy(ogT[:, 0:1024], pA[:, :]),
                      waits=(("st", lm[("ogt_mm", t)]),))
                ogt_ev = em.op("scalar", lambda eng: eng.copy(
                    ogT[:, 1024:2048], pD[:, :]), inc=("ss", 1))
                lm[("ogt_ev", t)] = ogt_ev
                w_s = [("ss", ogt_ev)] + lw("dss", ("s_store", t - 1))
                for s_ in range(16):
                    em.op("tensor", (lambda eng, s2=s_: eng.matmul(
                        pD[:, 0:512], ogT[:, s2 * 128:(s2 + 1) * 128],
                        wo[:, s2 * D:s2 * D + 512],
                        start=(s2 == 0), stop=(s2 == 15))),
                        waits=(w_s if s_ == 0 else ()))
                    em.op("tensor", (lambda eng, s2=s_: eng.matmul(
                        pD[:, 512:1024], ogT[:, s2 * 128:(s2 + 1) * 128],
                        wo[:, s2 * D + 512:(s2 + 1) * D],
                        start=(s2 == 0), stop=(s2 == 15))),
                        inc=("st", 1) if s_ == 15 else None)
                lm[("s_mm", t)] = em.val("st")
                lm[("s_ev", t)] = em.op(
                    "vector", (lambda eng: eng.tensor_copy(
                        sbo[:, :], pD[:, :])),
                    waits=(("st", lm[("s_mm", t)]),), inc=("sv", 1))
                lm[("s_store", t)] = em.op(
                    "scalar", (lambda eng, r=r0: eng.dma_start(
                        s_scr[r:r + 128, :], sbo[:, :])),
                    waits=(("sv", lm[("s_ev", t)]),), inc=("dss", 16))


            if sge("b1"):
                emit_ln(0)

            for t in range(n_tiles if sge("b1") else 0):
                tcur = tb[0]
                r0 = t * 128

                w_t = lw("sv", ("tfree", t - 1))
                tsem = f"dt{t % 2}"
                tin = em.op("sync",
                            (lambda eng, tc=tcur, r=r0:
                             eng.dma_start(tc[:, :], tabs_d[r:r + 128, :])),
                            waits=w_t, inc=(tsem, 16))

                if not sge("bb"):
                    continue
                hl_done = lm[("hl", t)]
                if dbg:
                    em.op("gpsimd", (lambda eng, r=r0: eng.dma_start(
                        xh_scr[r:r + 128, :], xh[:, :])),
                        waits=(("sv", hl_done),), inc=("dgp", 16))
                    em.op("gpsimd", (lambda eng, r=r0: eng.dma_start(
                        xl_scr[r:r + 128, :], junk[:, :])),
                        waits=(("sv", hl_done),), inc=("dgp", 16))
                w_tr = ([("sv", hl_done), ("dgp", w_loaded)]
                        + lw("ss", ("o_ev", t - 1, H - 1))
                        + lw("sv", ("at_ev", t - 1, H - 1))
                        + lw("ss", ("xnt_ev", t - 1))
                        + lw("sv", ("xntf_ev", t - 1)))
                for s in range(8):
                    em.op("tensor", (lambda eng, s_=s: eng.matmul(
                        pA[:, s_ * 128:(s_ + 1) * 128],
                        xh[:, s_ * 128:(s_ + 1) * 128], idb[:, :],
                        start=True, stop=False)),
                        waits=(w_tr if s == 0 else ()))
                    em.op("tensor", (lambda eng, s_=s: eng.matmul(
                        pA[:, s_ * 128:(s_ + 1) * 128],
                        junk[:, s_ * 128:(s_ + 1) * 128], idb[:, :],
                        start=False, stop=True)),
                        inc=("st", 1) if s == 7 else None)
                lm[("xnt_mm", t)] = em.val("st")
                if not sge("bc"):
                    continue
                w_evf = ([("st", lm[("xnt_mm", t)])]
                         + lw("dsx", ("xnts", t - 1)))
                xntf_ev = em.op("vector", lambda eng: eng.tensor_copy(
                    xn[:, :], pA[:, :]), waits=w_evf, inc=("sv", 1))
                lm[("xntf_ev", t)] = xntf_ev
                w_ev = ([("st", lm[("xnt_mm", t)]), ("sv", xntf_ev)]
                        + lw("st", ("g_mm", t - 1)))
                xnt_ev = em.op("scalar", lambda eng: eng.copy(
                    xntb[:, :], pA[:, :]), waits=w_ev, inc=("ss", 1))
                lm[("xnt_ev", t)] = xnt_ev
                if not sge("base"):
                    continue
                lm[("xnts", t)] = em.op("scalar",
                                        (lambda eng, r=r0: eng.dma_start(
                                            xnt_scr[r:r + 128, :],
                                            xn[:, :])),
                                        waits=(("sv", xntf_ev),),
                                        inc=("dsx", 16))

                if not sge("proj"):
                    continue
                def proj_round(wsrc, col0, name_mm, waits_first,
                               dlo=None, dhi=None):
                    dlo = dlo if dlo is not None else pB
                    dhi = dhi if dhi is not None else pC
                    for s in range(8):
                        base = col0 + s * (wsrc.shape[1] // 8)
                        for blk in range(4):
                            dst = (dlo if blk < 2 else dhi)
                            dcol = (blk % 2) * 512
                            em.op("tensor",
                                  (lambda eng, s_=s, b_=base, bl=blk, d=dst,
                                   dc=dcol, w_=wsrc: eng.matmul(
                                       d[:, dc:dc + 512],
                                       xntb[:, s_ * 128:(s_ + 1) * 128],
                                       w_[:, b_ + bl * 512:b_ + bl * 512 + 512],
                                       start=(s_ == 0), stop=(s_ == 7))),
                                  waits=(waits_first if (s == 0 and blk == 0)
                                         else ()),
                                  inc=(("st", 1) if (s == 7 and blk == 3)
                                       else None))
                    return em.val("st")

                w_qk = [("ss", xnt_ev)]
                if t == 0:
                    w_qk += [("dgp", w_loaded), ("sv", gnw_done)]
                else:
                    w_qk += lw("sv", ("s_ev", t - 1))
                for s in range(8):
                    for blk in range(4):
                        dst = pB if blk < 2 else pC
                        wsrc = wq if blk < 2 else wk
                        dcol = (blk % 2) * 512
                        scol = (blk % 2) * 512
                        em.op("tensor",
                              (lambda eng, s_=s, bl=blk, d=dst, dc=dcol,
                               w_=wsrc, sc=scol: eng.matmul(
                                   d[:, dc:dc + 512],
                                   xntb[:, s_ * 128:(s_ + 1) * 128],
                                   w_[:, s_ * D + sc:s_ * D + sc + 512],
                                   start=(s_ == 0), stop=(s_ == 7))),
                              waits=(w_qk if (s == 0 and blk == 0) else ()),
                              inc=(("st", 1) if (s == 7 and blk == 3)
                                   else None))
                qk_mm = em.val("st")

                if t + 1 < n_tiles:
                    emit_ln(t + 1)
                if t >= 1:
                    emit_ogT_s(t - 1)

                first_w = [("st", qk_mm), (tsem, tin), ("dcs", const_xa)]
                seq = [
                    (pB, qp, 0, 512, 0, 512, ALU.subtract, 0),
                    (pB, qp, 512, 0, 0, 512, ALU.add, 512),
                    (pC, kp, 0, 512, 1024, 1536, ALU.subtract, 0),
                    (pC, kp, 512, 0, 1024, 1536, ALU.add, 512),
                ]
                n_r = 0
                for (ps, dst, ac, bc, ct, st_, alu, oc) in seq:
                    em.op("vector", (lambda eng, p_=ps, a_=ac, c_=ct:
                                     eng.tensor_tensor(
                                         tmp1[:, :], p_[:, a_:a_ + 512],
                                         tcur[:, c_:c_ + 512], ALU.mult)),
                          waits=(first_w if n_r == 0 else ()))
                    n_r += 1
                    em.op("vector", (lambda eng, p_=ps, b_=bc, s2=st_:
                                     eng.tensor_tensor(
                                         tmp2[:, :], p_[:, b_:b_ + 512],
                                         tcur[:, s2:s2 + 512], ALU.mult)))
                    em.op("vector", (lambda eng, d=dst, o_=oc, a2=alu:
                                     eng.tensor_tensor(
                                         d[:, o_:o_ + 512], tmp1[:, :],
                                         tmp2[:, :], a2)),
                          inc=("sv", 1) if n_r == 4 else None)
                rope_done = em.val("sv")
                lm[("rope_done", t)] = rope_done
                lm[("tfree", t)] = rope_done

                if not sge("vg"):
                    continue
                w_v = ([("ss", xnt_ev), ("sv", xntf_ev)]
                       + lw("sv", ("s_ev", t - 1)))
                v_mm = proj_round(wv, 0, "v", w_v, dlo=pA, dhi=pD)
                w_vev = [("st", v_mm)] + lw("sv", ("og_done", t - 1))
                em.op("scalar", lambda eng: eng.copy(vb[:, 0:1024], pA[:, :]),
                      waits=w_vev)
                v_ev = em.op("scalar", lambda eng: eng.copy(
                    vb[:, 1024:2048], pD[:, :]), inc=("ss", 1))
                g_mm = proj_round(wg, 0, "g", [("ss", v_ev)], dlo=pA, dhi=pD)
                lm[("g_mm", t)] = g_mm
                w_m = [("st", g_mm)] + lw("sv", ("og_done", t - 1))
                em.op("scalar", lambda eng: eng.activation(
                    mb[:, 0:1024], pA[:, :], AF.Silu), waits=w_m)
                m_done = em.op("scalar", lambda eng: eng.activation(
                    mb[:, 1024:2048], pD[:, :], AF.Silu), inc=("ss", 1))
                lm[("m_done", t)] = m_done

                if not sge("tq"):
                    continue
                w_tq = [("sv", rope_done), ("dgp", w_loaded)]
                for s in range(8):
                    em.op("tensor", (lambda eng, s_=s: eng.matmul(
                        pB[:, s_ * 128:(s_ + 1) * 128],
                        qp[:, s_ * 128:(s_ + 1) * 128], idb[:, :],
                        start=True, stop=True)),
                        waits=(w_tq if s == 0 else ()))
                for s in range(8):
                    em.op("tensor", (lambda eng, s_=s: eng.matmul(
                        pC[:, s_ * 128:(s_ + 1) * 128],
                        kp[:, s_ * 128:(s_ + 1) * 128], idb[:, :],
                        start=True, stop=True)),
                        inc=("st", 1) if s == 7 else None)
                tqk_mm = em.val("st")
                em.op("scalar", lambda eng: eng.copy(qpT[:, :], pB[:, :]),
                      waits=(("st", tqk_mm),))
                qkT_ev = em.op("scalar", lambda eng: eng.copy(
                    kpT[:, :], pC[:, :]), inc=("ss", 1))
                lm[("qkT_ev", t)] = qkT_ev

                em.op("vector", lambda eng: eng.memset(gvar[:, :], 0.0),
                      waits=(("ss", qkT_ev),))

                if not sge("ret"):
                    continue
                for h in range(H):
                    gC = gCs[h]
                    w_P = [("ss", qkT_ev), ("ss", m_done)]
                    if h > 0:
                        w_P.append(("sv", lm[("tupd", t, h - 1)]))
                    em.op("tensor", (lambda eng, h_=h: eng.matmul(
                        pD[:, 0:512], kp[:, 128 * h_:128 * h_ + 128],
                        vb[:, 512 * h_:512 * h_ + 512],
                        start=True, stop=True)), waits=w_P)
                    em.op("tensor", (lambda eng, h_=h: eng.matmul(
                        pD[:, 512:1024],
                        kp[:, 512 + 128 * h_:512 + 128 * h_ + 128],
                        vb[:, 512 * h_:512 * h_ + 512],
                        start=True, stop=True)), inc=("st", 1))
                    P_mm = em.val("st")
                    w_at = []
                    if h > 0:
                        w_at.append(("sv", lm[("at_ev", t, h - 1)]))
                    else:
                        w_at += lw("ss", ("ogt_ev", t - 1))
                    em.op("tensor", (lambda eng, h_=h: eng.matmul(
                        pA[:, 0:128], kpT[:, 128 * h_:128 * h_ + 128],
                        qpT[:, 128 * h_:128 * h_ + 128],
                        start=True, stop=False)), waits=w_at)
                    em.op("tensor", (lambda eng, h_=h: eng.matmul(
                        pA[:, 0:128], kpT[:, 128 * (4 + h_):128 * (4 + h_) + 128],
                        qpT[:, 128 * (4 + h_):128 * (4 + h_) + 128],
                        start=False, stop=True)), inc=("st", 1))
                    at_mm = em.val("st")
                    w_sb = [("sv", tst_init)] if (t, h) == (0, 0) else \
                        [("st", lm[("o_mm", t, h - 1)] if h > 0
                          else lm[("o_mm", t - 1, H - 1)])]
                    em.op("scalar", (lambda eng, h_=h, g_=gC: eng.activation(
                        Sbf[:, 0:512], Tst[:, 1024 * h_:1024 * h_ + 512],
                        AF.Identity, scale=float(g_))), waits=w_sb)
                    sbf_done = em.op("scalar",
                                     (lambda eng, h_=h, g_=gC: eng.activation(
                                         Sbf[:, 512:1024],
                                         Tst[:, 1024 * h_ + 512:1024 * (h_ + 1)],
                                         AF.Identity, scale=float(g_))),
                                     inc=("ss", 1))
                    em.op("vector", (lambda eng, h_=h, g_=gC:
                                     eng.scalar_tensor_tensor(
                                         Tst[:, 1024 * h_:1024 * h_ + 512],
                                         Tst[:, 1024 * h_:1024 * h_ + 512],
                                         g_, pD[:, 0:512],
                                         ALU.mult, ALU.add)),
                          waits=(("st", P_mm), ("ss", sbf_done)))
                    lm[("tupd", t, h)] = em.op(
                        "vector", (lambda eng, h_=h, g_=gC:
                                   eng.scalar_tensor_tensor(
                                       Tst[:, 1024 * h_ + 512:1024 * (h_ + 1)],
                                       Tst[:, 1024 * h_ + 512:1024 * (h_ + 1)],
                                       g_, pD[:, 512:1024],
                                       ALU.mult, ALU.add)), inc=("sv", 1))
                    lm[("at_ev", t, h)] = em.op(
                        "vector", (lambda eng: eng.tensor_tensor(
                            atb[:, :], pA[:, 0:128], mskt[:, :], ALU.mult)),
                        waits=(("st", at_mm),), inc=("sv", 1))
                    w_o = [("ss", sbf_done), ("sv", lm[("at_ev", t, h)])]
                    if h > 0:
                        w_o.append(("ss", lm[("o_ev", t, h - 1)]))
                    em.op("tensor", (lambda eng, h_=h: eng.matmul(
                        pA[:, 512:1024], qpT[:, 128 * h_:128 * h_ + 128],
                        Sbf[:, 0:512], start=True, stop=False)), waits=w_o)
                    em.op("tensor", (lambda eng, h_=h: eng.matmul(
                        pA[:, 512:1024],
                        qpT[:, 128 * (4 + h_):128 * (4 + h_) + 128],
                        Sbf[:, 512:1024], start=False, stop=False)))
                    em.op("tensor", (lambda eng, h_=h: eng.matmul(
                        pA[:, 512:1024], atb[:, :],
                        vb[:, 512 * h_:512 * h_ + 512],
                        start=False, stop=True)), inc=("st", 1))
                    lm[("o_mm", t, h)] = em.val("st")
                    em.op("scalar", (lambda eng, h_=h: eng.activation(
                        junk2[:, :], pA[:, 512:1024], AF.Square,
                        accum_out=gvar[:, h_:h_ + 1])),
                        waits=(("st", lm[("o_mm", t, h)]),))
                    w_oev = (lw("sv", ("og_done", t - 1))
                             if h == 0 else [])
                    lm[("o_ev", t, h)] = em.op(
                        "scalar", (lambda eng, h_=h: eng.copy(
                            obf[:, 512 * h_:512 * h_ + 512],
                            pA[:, 512:1024])), waits=w_oev, inc=("ss", 1))
                lm[("qkT_free", t)] = lm[("at_ev", t, H - 1)]

                if not sge("wo"):
                    continue
                gv_done = em.op("vector", lambda eng: eng.tensor_scalar(
                    gv4[:, :], gvar[:, :], 1.0 / HV, EPS, ALU.mult, ALU.add),
                    waits=(("ss", lm[("o_ev", t, H - 1)]),), inc=("sv", 1))
                rtg_done = em.op("scalar", lambda eng: eng.activation(
                    rtg[:, :], gv4[:, :], AF.Sqrt),
                    waits=(("sv", gv_done),), inc=("ss", 1))
                rsg_done = em.op("vector", lambda eng: eng.reciprocal(
                    rsg[:, :], rtg[:, :]), waits=(("ss", rtg_done),),
                    inc=("sv", 1))
                og_done = None
                for h in range(H):
                    og_done = em.op(
                        "vector", (lambda eng, h_=h: eng.scalar_tensor_tensor(
                            ogb[:, 512 * h_:512 * h_ + 512],
                            obf[:, 512 * h_:512 * h_ + 512],
                            rsg[:, h_:h_ + 1],
                            mb[:, 512 * h_:512 * h_ + 512],
                            ALU.mult, ALU.mult)),
                        waits=((("sv", rsg_done), ("ss", m_done))
                               if h == 0 else ()),
                        inc=("sv", 1) if h == H - 1 else None)
                lm[("og_done", t)] = og_done


            if n_tiles >= 1 and sge("wo"):
                emit_ogT_s(n_tiles - 1, last=True)


            do_cc = sge("cc")
            am_red = (em.op("vector", (lambda eng: eng.reduce_max(
                stats[:, 8:9], amaxb[:, 0:n_tiles], AX.X)), inc=("sv", 1))
                if do_cc else None)
            g1 = (em.op("gpsimd", (lambda eng: eng.dma_start(
                red_scr[:, :], stats[:, 8:9])),
                waits=(("sv", am_red),), inc=("dgp", 16)) if do_cc else None)
            g2 = (em.op("gpsimd", (lambda eng: eng.dma_start(
                rowbuf[:, :], bass.AP(red_scr, 0, [[1, 1], [1, 128]]))),
                waits=(("dgp", g1),), inc=("dgp", 16)) if do_cc else None)
            am_sc = (em.op("vector", (lambda eng: eng.reduce_max(
                stats[0:1, 8:9], rowbuf[:, :], AX.X)),
                waits=(("dgp", g2),), inc=("sv", 1)) if do_cc else None)
            cc_st = (em.op("gpsimd", (lambda eng: eng.dma_start(
                cc_in[:, :], stats[0:1, 8:9])),
                waits=(("sv", am_sc),), inc=("dgp", 16)) if do_cc else None)
            cc_done = (em.op("gpsimd", (lambda eng: eng.collective_compute(
                "AllReduce", ALU.max,
                replica_groups=[list(range(n_cores))],
                ins=[cc_in.ap().opt()], outs=[cc_out.ap().opt()])),
                waits=(("dgp", cc_st),), inc=("sq", 1)) if do_cc else None)

            p1_final = [(k, em.val(k)) for k in
                        ("dcs", "dx0", "dx1", "dt0", "dt1", "dsx", "dss",
                         "dgp", "sv", "ss", "st") if em.val(k) > 0]

            em.replay_block(nc, sems)

        if not sge("full"):
            emx = Em(em.cnt)
            emx.op("gpsimd", (lambda eng, xb0=xb[0]: eng.dma_start(
                xb0[:, :], x_d[0:128, :])),
                waits=p1_final, inc=("dgp", 16))
            emx.op("gpsimd", (lambda eng, xb0=xb[0]: eng.dma_start(
                out_d[0:128, :], xb0[:, :])),
                waits=(("dgp", emx.val("dgp")),), inc=("dgp", 16))
            emx.op("gpsimd", lambda eng: None,
                   waits=[(k, emx.val(k)) for k in
                          ("dgp", "sv", "ss", "st", "dcs", "dx0", "dx1",
                           "dt0", "dt1", "dsx", "dss") if emx.val(k) > 0])
            emx.replay_block(nc, sems)
            return nc

        em2 = Em(em.cnt)
        lm2 = {}
        with ExitStack() as p2:
            e2 = p2.enter_context
            wfst = e2(nc.sbuf_tensor("wfst", [128, 8 * D], F32))
            wq8 = e2(nc.sbuf_tensor("wq8", [128, 8 * D], FP8))
            bias_t = e2(nc.sbuf_tensor("bias_t", [128, D], F32))
            xnt2 = [e2(nc.sbuf_tensor(f"xnt2_{i}", [128, D], F32))
                    for i in range(2)]
            xb2 = [e2(nc.sbuf_tensor(f"xb2_{i}", [128, D], F32))
                   for i in range(2)]
            sb2 = [e2(nc.sbuf_tensor(f"sb2_{i}", [128, D], BF16))
                   for i in range(2)]
            xq8 = [e2(nc.sbuf_tensor(f"xq8_{i}", [128, D], FP8))
                   for i in range(2)]
            ysb = [e2(nc.sbuf_tensor(f"ysb_{i}", [128, D], F32))
                   for i in range(2)]
            gff = [e2(nc.sbuf_tensor(f"gff_{i}", [128, D], BF16))
                   for i in range(2)]
            prod = [e2(nc.sbuf_tensor(f"prod_{i}", [128, D], F32))
                    for i in range(2)]
            outb = [e2(nc.sbuf_tensor(f"outb_{i}", [128, D], F32))
                    for i in range(2)]
            ps_y = [e2(nc.psum_tensor(f"ps_y{i}", [128, D], F32))
                    for i in range(2)]

            for s in range(8):
                em2.op("gpsimd", (lambda eng, s_=s: eng.dma_start(
                    wfst[:, s_ * D:(s_ + 1) * D],
                    wft_d[s_ * 128:(s_ + 1) * 128, :])),
                    waits=(p1_final if s == 0 else ()), inc=("dgp", 16))
            wf_ld = em2.val("dgp")

            amw_red = em2.op("vector", (lambda eng: eng.reduce_max(
                stats[:, 13:14], wfst[:, :], AX.X,
                apply_absolute_value=True)),
                waits=[("dgp", wf_ld)] + p1_final, inc=("sv", 1))
            h1 = em2.op("gpsimd", (lambda eng: eng.dma_start(
                red_scr2[:, :], stats[:, 13:14])),
                waits=(("sv", amw_red),), inc=("dgp", 16))
            h2 = em2.op("gpsimd", (lambda eng: eng.dma_start(
                rowbuf[:, :], bass.AP(red_scr2, 0, [[1, 1], [1, 128]]))),
                waits=(("dgp", h1),), inc=("dgp", 16))
            amw_sc = em2.op("vector", (lambda eng: eng.reduce_max(
                stats[0:1, 13:14], rowbuf[:, :], AX.X)),
                waits=(("dgp", h2),), inc=("sv", 1))
            h3 = em2.op("gpsimd", (lambda eng: eng.dma_start(
                wmx_scr[:, :], stats[0:1, 13:14])),
                waits=(("sv", amw_sc),), inc=("dgp", 16))
            h4 = em2.op("gpsimd", (lambda eng: eng.dma_start(
                amaxb[:, 1:2], bass.AP(wmx_scr, 0, [[0, 128], [1, 1]]))),
                waits=(("dgp", h3),), inc=("dgp", 16))
            amg_bc = em2.op("gpsimd", (lambda eng: eng.dma_start(
                stats[:, 9:10], bass.AP(cc_out, 0, [[0, 128], [1, 1]]))),
                waits=(("sq", cc_done), ("dgp", h4)), inc=("dgp", 16))
            k1 = em2.op("vector", (lambda eng: eng.tensor_scalar(
                stats[:, 9:10], stats[:, 9:10], 1e-12, None, ALU.max)),
                waits=(("dgp", amg_bc),), inc=("sv", 1))
            k2 = em2.op("vector", (lambda eng: eng.reciprocal(
                stats[:, 10:11], stats[:, 9:10])), waits=(("sv", k1),),
                inc=("sv", 1))
            k3 = em2.op("vector", (lambda eng: eng.tensor_scalar(
                stats[:, 10:11], stats[:, 10:11], 224.0, None, ALU.mult)),
                waits=(("sv", k2),), inc=("sv", 1))
            k4 = em2.op("vector", (lambda eng: eng.tensor_scalar(
                amaxb[:, 1:2], amaxb[:, 1:2], 1e-12, None, ALU.max)),
                waits=(("sv", k3),), inc=("sv", 1))
            k5 = em2.op("vector", (lambda eng: eng.reciprocal(
                stats[:, 11:12], amaxb[:, 1:2])), waits=(("sv", k4),),
                inc=("sv", 1))
            k6 = em2.op("vector", (lambda eng: eng.tensor_scalar(
                stats[:, 11:12], stats[:, 11:12], 224.0, None, ALU.mult)),
                waits=(("sv", k5),), inc=("sv", 1))
            k7 = em2.op("vector", (lambda eng: eng.tensor_tensor(
                stats[:, 12:13], stats[:, 10:11], stats[:, 11:12], ALU.mult)),
                waits=(("sv", k6),), inc=("sv", 1))
            sc_done = em2.op("vector", (lambda eng: eng.reciprocal(
                stats[:, 12:13], stats[:, 12:13])), waits=(("sv", k7),),
                inc=("sv", 1))
            wq8_done = em2.op("scalar", (lambda eng: eng.activation(
                wq8[:, :], wfst[:, :], AF.Identity, scale=stats[:, 11:12])),
                waits=[("sv", sc_done)] + p1_final, inc=("ss", 1))
            bias_bc = em2.op("gpsimd", (lambda eng: eng.dma_start(
                bias_t[:, :], bass.AP(bias_d, 0, [[0, 128], [1, D]]))),
                waits=tuple(p1_final), inc=("dgp", 16))

            def emit_p2_in(t):
                i = t % 2
                r0 = t * 128
                asem, bsem, psem = f"da{t % 2}", f"db{t % 2}", f"dp{t % 2}"
                xnt_in = em2.op("sync", (lambda eng, i_=i, r=r0: eng.dma_start(
                    xnt2[i_][:, :], xnt_scr[r:r + 128, :])),
                    waits=([("ss", lm2[("xq", t - 2)])] if t >= 2
                           else [("sv", sc_done)]),
                    inc=(asem, 16))
                x_in = em2.op("sync", (lambda eng, i_=i, r=r0: eng.dma_start(
                    xb2[i_][:, :], x_d[r:r + 128, :])),
                    waits=([("sv", lm2[("comb", t - 2)])] if t >= 2 else []),
                    inc=(bsem, 16))
                s_in = em2.op("gpsimd", (lambda eng, i_=i, r=r0: eng.dma_start(
                    sb2[i_][:, :], s_scr[r:r + 128, :])),
                    waits=([("sv", lm2[("prod", t - 2)])] if t >= 2 else []),
                    inc=(psem, 16))
                lm2[("xq", t)] = em2.op("scalar",
                                        (lambda eng, i_=i: eng.activation(
                                            xq8[i_][:, :], xnt2[i_][:, :],
                                            AF.Identity,
                                            scale=stats[:, 10:11])),
                                        waits=((asem, xnt_in),
                                               ("ss", wq8_done)),
                                        inc=("ss", 1))
                lm2[("xin", t)] = x_in
                lm2[("sin", t)] = s_in

            if sge("b1") and n_tiles > 0:
                emit_p2_in(0)
            for t in range(n_tiles if sge("b1") else 0):
                i = t % 2
                r0 = t * 128
                asem, bsem, psem, osem = (f"da{i}", f"db{i}", f"dp{i}",
                                          f"do{i}")
                x_in = lm2[("xin", t)]
                s_in = lm2[("sin", t)]
                w_mm = [("ss", lm2[("xq", t)])]
                if t >= 2:
                    w_mm.append(("sv", lm2[("yev", t - 2)]))
                for s in range(8):
                    em2.op("tensor", (lambda eng, i_=i, s_=s: eng.matmul(
                        ps_y[i_][:, 0:512],
                        xq8[i_][:, s_ * 128:(s_ + 1) * 128],
                        wq8[:, s_ * D:s_ * D + 512],
                        start=(s_ == 0), stop=(s_ == 7))),
                        waits=(w_mm if s == 0 else ()))
                    em2.op("tensor", (lambda eng, i_=i, s_=s: eng.matmul(
                        ps_y[i_][:, 512:1024],
                        xq8[i_][:, s_ * 128:(s_ + 1) * 128],
                        wq8[:, s_ * D + 512:(s_ + 1) * D],
                        start=(s_ == 0), stop=(s_ == 7))),
                        inc=("st", 1) if s == 7 else None)
                y_mm = em2.val("st")
                if t + 1 < n_tiles:
                    emit_p2_in(t + 1)
                lm2[("yev", t)] = em2.op(
                    "vector", (lambda eng, i_=i: eng.scalar_tensor_tensor(
                        ysb[i_][:, :], ps_y[i_][:, :], stats[:, 12:13],
                        bias_t[:, :], ALU.mult, ALU.add)),
                    waits=(("st", y_mm), ("dgp", bias_bc)), inc=("sv", 1))
                gff_done = em2.op("scalar", (lambda eng, i_=i: eng.activation(
                    gff[i_][:, :], ysb[i_][:, :], AF.Silu)),
                    waits=(("sv", lm2[("yev", t)]),), inc=("ss", 1))
                lm2[("prod", t)] = em2.op(
                    "vector", (lambda eng, i_=i: eng.tensor_tensor(
                        prod[i_][:, :], gff[i_][:, :], sb2[i_][:, :],
                        ALU.mult)),
                    waits=(("ss", gff_done), (psem, s_in)), inc=("sv", 1))
                lm2[("comb", t)] = em2.op(
                    "vector", (lambda eng, i_=i: eng.tensor_tensor(
                        outb[i_][:, :], xb2[i_][:, :], prod[i_][:, :],
                        ALU.add)),
                    waits=((bsem, x_in), ("sv", lm2[("prod", t)]),
                           ((osem, lm2[("ost", t - 2)]) if t >= 2
                            else ("sv", 0))),
                    inc=("sv", 1))
                lm2[("ost", t)] = em2.op(
                    "scalar", (lambda eng, i_=i, r=r0: eng.dma_start(
                        out_d[r:r + 128, :], outb[i_][:, :])),
                    waits=(("sv", lm2[("comb", t)]),), inc=(osem, 16))

            em2.op("scalar", lambda eng: None,
                   waits=(("do0", em2.val("do0")), ("do1", em2.val("do1"))))
            sv_f = em2.op("vector", lambda eng: eng.engine_nop(),
                          inc=("sv", 1))
            ss_f = em2.op("scalar", lambda eng: eng.copy(
                stats[0:1, 0:1], stats[0:1, 0:1]), inc=("ss", 1))
            em2.op("gpsimd", lambda eng: None,
                   waits=[("sv", sv_f), ("ss", ss_f), ("st", em2.val("st"))]
                   + [(k, em2.val(k)) for k in
                      ("da0", "da1", "db0", "db1", "dp0", "dp1",
                       "do0", "do1", "dgp") if em2.val(k) > 0])
            em2.replay_block(nc, sems)

    return nc


def _prep_inputs(Wq, Wk, Wv, Wg, Wo, gnw, Wf, bias, n_tiles):
    def perm_qk(W):
        Wp = W.reshape(H, 2, 128, D).transpose(1, 0, 2, 3).reshape(D, D)
        return np.ascontiguousarray(Wp.T)
    return {
        "wqt": perm_qk(np.asarray(Wq)),
        "wkt": perm_qk(np.asarray(Wk)),
        "wvt": np.ascontiguousarray(np.asarray(Wv).T),
        "wgt": np.ascontiguousarray(np.asarray(Wg).T),
        "wot": np.ascontiguousarray(np.asarray(Wo).T),
        "wft": np.ascontiguousarray(np.asarray(Wf).T),
        "fbias": np.asarray(bias).reshape(1, D).astype(np.float32),
        "gnw": np.ascontiguousarray(
            np.asarray(gnw).reshape(4, 128).T.astype(np.float32)),
        "idf": np.eye(128, dtype=np.float32),
        "mskt": np.triu(np.ones((128, 128), np.float32)),
        "tabs": host_tables(n_tiles * C),
    }


def kernel(L0_attn_x, L0_attn_Wq, L0_attn_Wk, L0_attn_Wv, L0_attn_Wg,
           L0_attn_Wo, L0_attn_gnorm_w, L0_ffn_weight, L0_ffn_bias):
    n_tiles = 64
    n_cores = 8
    x = np.asarray(L0_attn_x)
    base = _prep_inputs(L0_attn_Wq, L0_attn_Wk, L0_attn_Wv, L0_attn_Wg,
                        L0_attn_Wo, L0_attn_gnorm_w, L0_ffn_weight,
                        L0_ffn_bias, n_tiles)
    nc = build(n_tiles, n_cores)
    in_maps = [dict(base, x=np.ascontiguousarray(x[b]))
               for b in range(n_cores)]
    res = run_bass_kernel_spmd(nc, in_maps, core_ids=list(range(n_cores)))
    out = np.stack([res.results[b]["out"] for b in range(n_cores)], axis=0)
    return out.astype(np.float32)
